# version 32
# baseline (speedup 1.0000x reference)
import contextlib

import numpy as np
import ml_dtypes

import concourse.bass as bass
import concourse.mybir as mybir
from concourse.ap import AP
from concourse.bass_utils import run_bass_kernel_spmd
import concourse.dve_ops as dve_ops
from concourse.dve_op import DveOp
from concourse.dve_spec import Spec, Src0, Src1, C0, C1, C2, C3, Zero, One, lower, maxx

F32 = mybir.dt.float32
BF16 = mybir.dt.bfloat16
U8 = mybir.dt.uint8
AF = mybir.ActivationFunctionType
ALU = mybir.AluOpType
AX = mybir.AxisListType

NCORE = 8
Bb, S, D = 2, 8192, 1024
H, KVH, HD = 16, 4, 64
WIN = 256
II = 2816
TO = 2048
HALO = 256
TT = TO + HALO
NT = TT // 128
NQ = TO // 128
IHALF = II // 2
NCH = IHALF // 128
WG_SH = II // NCORE
WD_SH = D // NCORE
PAIRS = [(2 * i, 2 * i + 1) for i in range(8)]

_REG = {}


def _mk_op(name, spec, reference, steal_from):
    if name in _REG:
        return _REG[name]
    shas = {}
    for v in ("v3", "v4"):
        try:
            shas[v] = lower(spec, ver=v, perf_en=False, name=name).sha(v)
        except Exception:
            pass
    op = DveOp(name, spec, subdim=False, uops_sha=shas, reference=reference)
    dve_ops.OPS.append(op)
    dve_ops._SUB_OPCODE_FOR_NAME[name] = dve_ops._SUB_OPCODE_FOR_NAME[steal_from]
    dve_ops.CUSTOM_DVE_SPECS[name] = spec
    dve_ops.CUSTOM_DVE_REFS[name] = reference
    _REG[name] = op
    return op


def _f32(x):
    return np.asarray(x, np.float32)


T_ABS = _mk_op(
    "ANT_T_ABS", Spec(body=maxx(Src0 * Src1, Zero - Src0 * Src1)),
    lambda in0, in1, s0, s1, imm2: _f32(np.abs(_f32(in0) * _f32(in1))),
    "BITWISE_XOR_TT_BV")
SNAP_LO = _mk_op(
    "ANT_SNAP_LO", Spec(body=(Src0 > C0) + (Src0 > C1) + (Src0 > C2) + (Src0 > C3)),
    lambda in0, in1, s0, s1, imm2: _f32(
        _f32(in0 > s0) + _f32(in0 > s1) + _f32(in0 > imm2) + _f32(in0 > in1)),
    "DVEHASH32L_SEED")
_c5 = (Src0 > C0); _c6 = (Src0 > C1); _c7 = (Src0 > C2)
_s1 = _c5 + _c6; _s2 = _s1 + _c7; _s3 = _s2 + _c7; _s4 = _s3 + _s3
SNAP_HI = _mk_op(
    "ANT_SNAP_HI", Spec(body=_s4 + Src1),
    lambda in0, in1, s0, s1, imm2: _f32(
        2.0 * (_f32(in0 > s0) + _f32(in0 > s1) + 2.0 * _f32(in0 > imm2)) + _f32(in1)),
    "DVEHASH32L_BODY")
FINAL_SIGN = _mk_op(
    "ANT_FINAL_SIGN2", Spec(body=Src0 * ((Src1 >= Zero) + (Src1 >= Zero) - One)),
    lambda in0, in1, s0, s1, imm2: _f32(
        _f32(in0) * (2.0 * _f32(in1 >= 0) - 1.0)),
    "DVEHASH32L_DRAIN")
PMIX = _mk_op(
    "ANT_PMIX", Spec(body=Src0 * C0 + Src1 * C1),
    lambda in0, in1, s0, s1, imm2: _f32(_f32(in0) * _f32(s0) + _f32(in1) * _f32(s1)),
    "CODY_WAITE_CASCADE")


class Prog:
    ENGINES = ("pe", "dve", "act", "pool", "sp")
    QUEUES = ("pe", "dve", "act", "pool", "sp", "cc")

    def __init__(self):
        self.ops = []

    def add(self, engine, emit, reads=(), writes=(), dma=False, q=None, tag=""):
        self.ops.append(dict(e=engine, f=emit, r=tuple(reads), w=tuple(writes),
                             dma=dma, q=(q or engine), tag=tag, deps=set(),
                             inc=False, cnt=None, waits=[]))

    def schedule(self):
        lw, rd = {}, {}
        for i, op in enumerate(self.ops):
            deps = set()
            for r in op["r"]:
                if r in lw:
                    deps.add(lw[r])
            for w in op["w"]:
                if w in lw:
                    deps.add(lw[w])
                for j in rd.get(w, ()):
                    deps.add(j)
            deps.discard(i)
            op["deps"] = deps
            for r in op["r"]:
                rd.setdefault(r, []).append(i)
            for w in op["w"]:
                lw[w] = i
                rd[w] = []
        for op in self.ops:
            for j in op["deps"]:
                self.ops[j]["inc"] = True
        ccnt = {e: 0 for e in self.ENGINES}
        qcnt = {qn: 0 for qn in self.QUEUES}
        for op in self.ops:
            if op["dma"]:
                qcnt[op["q"]] += 16
                op["cnt"] = qcnt[op["q"]]
                op["inc"] = True
            elif op["inc"]:
                ccnt[op["e"]] += 1
                op["cnt"] = ccnt[op["e"]]
        seen = {e: {} for e in self.ENGINES}
        for op in self.ops:
            mx = {}
            for j in sorted(op["deps"]):
                pj = self.ops[j]
                key = ("d", pj["q"]) if pj["dma"] else ("c", pj["e"])
                if pj["cnt"] > mx.get(key, 0):
                    mx[key] = pj["cnt"]
            s = seen[op["e"]]
            for key, val in sorted(mx.items()):
                if s.get(key, 0) < val:
                    s[key] = val
                    op["waits"].append((key, val))


def _build_program():
    nc = bass.Bass("TRN2", target_bir_lowering=False)
    p = Prog()

    x_in = nc.dram_tensor("x_in", [TT, D], F32, kind="ExternalInput")
    wq_t = nc.dram_tensor("wq_t", [D, D], F32, kind="ExternalInput")
    wk_t = nc.dram_tensor("wk_t", [D, KVH * HD], F32, kind="ExternalInput")
    wv_t = nc.dram_tensor("wv_t", [D, KVH * HD], F32, kind="ExternalInput")
    wo_t = nc.dram_tensor("wo_t", [D, D], F32, kind="ExternalInput")
    wg_in = nc.dram_tensor("wg_in", [II, D], F32, kind="ExternalInput")
    wu_in = nc.dram_tensor("wu_in", [II, D], F32, kind="ExternalInput")
    wd_in = nc.dram_tensor("wd_in", [D, II], F32, kind="ExternalInput")
    qg_in = nc.dram_tensor("qg_in", [1, H], F32, kind="ExternalInput")
    pm_in = nc.dram_tensor("pm_in", [1, 32], F32, kind="ExternalInput")
    cos_in = nc.dram_tensor("cos_in", [TT, 32], F32, kind="ExternalInput")
    sin_in = nc.dram_tensor("sin_in", [TT, 32], F32, kind="ExternalInput")
    msk_in = nc.dram_tensor("msk_in", [3, 128, 384], BF16, kind="ExternalInput")
    idf_in = nc.dram_tensor("idf_in", [128, 128], F32, kind="ExternalInput")
    idb_in = nc.dram_tensor("idb_in", [128, 128], BF16, kind="ExternalInput")
    y_out = nc.dram_tensor("y_out", [TO, D], F32, kind="ExternalOutput")

    dbg = "ExternalOutput" if _DEBUG else "Internal"
    qT_d = nc.dram_tensor("qT_d", [8, 128, TO], BF16)
    kT_d = nc.dram_tensor("kT_d", [4, 128, TT], BF16)
    v_d = nc.dram_tensor("v_d", [NT, 128, 260], BF16)
    xqT_d = nc.dram_tensor("xqT_d", [8, 128, TO], BF16)
    acc_d = nc.dram_tensor("acc_d", [TO, D], F32, kind=dbg)
    acc2_d = nc.dram_tensor("acc2_d", [TO, D], F32, kind=dbg)
    xq_dbg = (nc.dram_tensor("xq_dbg", [TO, D], BF16, kind="ExternalOutput")
              if _DEBUG else None)
    ym_dbg = (nc.dram_tensor("ym_dbg", [TO, D], BF16, kind="ExternalOutput")
              if _DEBUG else None)
    e2_dbg = (nc.dram_tensor("e2_dbg", [4, 128, 1536], BF16, kind="ExternalOutput")
              if _DEBUG else None)
    qk_dbg = (nc.dram_tensor("qk_dbg", [2, 128, 1536], BF16, kind="ExternalOutput")
              if _DEBUG else None)
    wgq_d = nc.dram_tensor("wgq_d", [D, II], BF16)
    wuq_d = nc.dram_tensor("wuq_d", [D, II], BF16)
    wdq_d = nc.dram_tensor("wdq_d", [II, D], BF16)

    pb = [nc.alloc_psum_tensor(f"pb{i}", [128, 512], F32) for i in range(8)]
    pbb = [t.bitcast(BF16) for t in pb]
    PS = [f"ps{i}" for i in range(8)]

    def sb(name, shape, dtype):
        return nc.alloc_sbuf_tensor(name, shape, dtype)

    ident_f = sb("ident_f", [128, 128], F32)
    ident_b = sb("ident_b", [128, 128], BF16)
    msk_sb = sb("msk_sb", [128, 3 * 384], BF16)
    gb_sb = sb("gb_sb", [128, 32], F32)
    pm_sb = sb("pm_sb", [128, 32], F32)
    row_sb = sb("row_sb", [1, 64], F32)
    ones1 = sb("ones1", [1, 128], F32)
    c175 = sb("c175", [128, 1], F32)
    eps5 = sb("eps5", [128, 1], F32)
    eps6 = sb("eps6", [128, 1], F32)
    lnh = sb("lnh", [128, 1], F32)
    QFD = IHALF
    t_s = sb("t_s", [128, QFD], F32)
    qlo_s = sb("qlo_s", [128, QFD], F32)
    q2_s = t_s
    dq_s = qlo_s

    bm_s = sb("bm_s", [128, QFD // 16], F32)
    sc_s = sb("sc_s", [128, QFD // 16], F32)
    sh_s = sb("sh_s", [128, QFD // 16], F32)
    r_s = sb("r_s", [128, QFD // 16], F32)
    stage = [sb(f"stage{i}", [128, 1024], F32) for i in range(2)]
    xres_sb = [sb(f"xres{i}", [128, 1024], F32) for i in range(2)]
    ostage = [sb(f"ostage{i}", [128, 1024], F32) for i in range(2)]
    bn_s = sb("bn_s", [128, 12], F32)
    mv_s = sb("mv_s", [128, 2], F32)
    r1_s = sb("r1_s", [128, 1], F32)
    r2_s = sb("r2_s", [128, 1], F32)
    nm_s = sb("nm_s", [128, 1], F32)
    rk_s = sb("rk_s", [128, 20], F32)
    rk2_s = sb("rk2_s", [128, 20], F32)

    kt_sb = [sb(f"kt_sb{i}", [128, 1536], BF16) for i in range(2)]
    v2_sb = [sb(f"v2_sb{i}", [128, 780], BF16) for i in range(2)]


    rp_sb = sb("rp_sb", [128, 16], F32)
    sall_sb = sb("sall_sb", [128, 32], F32)

    UNION_BYTES = 139 * 1024
    union = nc.alloc_sbuf_tensor("union", [128, UNION_BYTES], U8)
    ubase = nc.lookup_mloc(union).addr

    class Ar:
        def __init__(self):
            self.off = 0
            self.keys = []

        def t(self, name, shape, dtype, key=None):
            per = int(np.prod(shape[1:])) * mybir.dt.size(dtype)
            h = nc.virtual_sbuf_tensor(name, shape, dtype)
            nc._tensor(h.name, h.shape, h.dtype, type="SB")
            mloc = nc.lookup_mloc(h)
            mloc.allocated = True
            mloc.addr = ubase + self.off
            self.off = (self.off + per + 31) & ~31
            assert self.off <= UNION_BYTES, f"arena overflow {self.off}"
            self.keys.append(key or name)
            return h

    arA = Ar()
    cosr = [arA.t(f"cosr{i}", [128, 32], F32, f"cos{i}") for i in range(4)]
    sinr = [arA.t(f"sinr{i}", [128, 32], F32, f"sin{i}") for i in range(4)]
    xn_t = [arA.t(f"xn{i}", [128, 1024], F32) for i in range(2)]
    xnT_t = [arA.t(f"xnT{i}", [128, 1024], BF16) for i in range(3)]
    xq_t = [arA.t(f"xq{i}", [128, 1024], BF16) for i in range(2)]
    xqT_t = [arA.t(f"xqT{i}", [128, 1024], BF16) for i in range(2)]
    sq_t = [arA.t(f"sqt{i}", [128, 1280], BF16, f"sq{i}") for i in range(2)]
    qn_t = [arA.t(f"qn{i}", [128, 1024], F32) for i in range(2)]
    qr_t = [arA.t(f"qr{i}", [128, 1024], BF16) for i in range(2)]
    qT_t = [arA.t(f"qTt{i}", [128, 1024], BF16, f"qT{i}") for i in range(2)]
    rt1 = arA.t("rt1", [128, 512], F32)
    rt2 = arA.t("rt2", [128, 512], F32)
    kn_t = [arA.t(f"kn{i}", [128, 256], F32) for i in range(2)]
    kr_t = [arA.t(f"kr{i}", [128, 256], BF16) for i in range(2)]
    krd_t = [arA.t(f"krd{i}", [128, 512], BF16, f"krd{i}") for i in range(2)]
    kT_t = [arA.t(f"kTt{i}", [128, 512], BF16, f"kT{i}") for i in range(2)]
    v_t = [arA.t(f"vt{i}", [128, 260], BF16, f"v{i}") for i in range(2)]
    wqT_sb = arA.t("wqT_sb", [128, 8 * 1024], BF16)
    wkT_sb = arA.t("wkT_sb", [128, 8 * 256], BF16)
    wvT_sb = arA.t("wvT_sb", [128, 8 * 256], BF16)
    woT_sb = sb("woT_sb", [128, 8 * 1024], BF16)
    qt_sb = [arA.t(f"qt_sb{i}", [128, 1024], BF16, f"qts{i}") for i in range(2)]
    y65_sb = [arA.t(f"y65_sb{i}", [128, 1040], F32, f"y65{i}") for i in range(2)]
    ymix_sb = [arA.t(f"ymix_sb{i}", [128, 1024], BF16, f"ymix{i}") for i in range(2)]
    ymixT_sb = [arA.t(f"ymixT_sb{i}", [128, 1024], BF16, f"ymixT{i}")
                for i in range(2)]
    e_sb = [[arA.t(f"e_sb{i}_{j}", [128, 1536], BF16, f"e{i}{j}")
             for j in range(2)] for i in range(2)]
    t_w = arA.t("t_w", [128, QFD], F32)
    qlo_w = arA.t("qlo_w", [128, QFD], F32)
    bm_w = arA.t("bm_w", [128, QFD // 16], F32)
    sc_w = arA.t("sc_w", [128, QFD // 16], F32)
    sh_w = arA.t("sh_w", [128, QFD // 16], F32)
    r_w = arA.t("r_w", [128, QFD // 16], F32)
    wsq_sb = arA.t("wsq_sb", [128, 2816], BF16)
    wstg = arA.t("wstg", [128, 1408], F32)
    wtT_sb = arA.t("wtT_sb", [128, 1024], BF16)

    arB = Ar()
    wg_sb = arB.t("wg_sb", [128, 8 * IHALF], BF16)
    wu_sb = arB.t("wu_sb", [128, 8 * IHALF], BF16)
    wd_sb = arB.t("wd_sb", [128, NCH * 1024], BF16)
    xqs_sb = [arB.t(f"xqs{i}", [128, 1024], BF16) for i in range(2)]
    ain_sb = [arB.t(f"ain{i}", [128, 1024], F32) for i in range(2)]
    sg_sb = [arB.t(f"sg_sb{i}", [128, IHALF], F32) for i in range(2)]
    h_sb = [arB.t(f"h_sb{i}", [128, IHALF], F32) for i in range(4)]
    hq_sb = [arB.t(f"hq_sb{i}", [128, IHALF], BF16) for i in range(2)]
    hqT_sb = [arB.t(f"hqT_sb{i}", [128, IHALF], BF16) for i in range(2)]

    def dma(qe, out_ap, in_ap, reads, writes, tag=""):
        p.add(qe, lambda e, o=out_ap, i=in_ap: e.dma_start(o, i),
              reads=reads, writes=writes, dma=True, tag=tag)

    def quant_chain(src_ap, out_ap, fd, src_key, out_key, pr=128, deq_eng="dve",
                    scr="a"):
        nb = fd // 16
        if scr == "w":
            tS, qloS, q2S, dqS = t_w, qlo_w, t_w, qlo_w
            bmS, scS, shS, rS = bm_w, sc_w, sh_w, r_w
            kT, kQ = "t_w", "qlo_w"
            kB, kC, kH, kR = "bm_w", "sc_w", "sh_w", "r_w"
        else:
            tS, qloS, q2S, dqS = t_s, qlo_s, q2_s, dq_s
            bmS, scS, shS, rS = bm_s, sc_s, sh_s, r_s
            kT, kQ = "t_s", "qlo_s"
            kB, kC, kH, kR = "bm_s", "sc_s", "sh_s", "r_s"
        src3 = AP(src_ap.tensor, src_ap.offset,
                  [list(src_ap.ap[0]), [16, nb], [1, 16]])
        p.add("dve", lambda e, s=src3, nb=nb, pr=pr: e.tensor_reduce(
            bmS[0:pr, 0:nb], s, AX.X, ALU.max, apply_absolute_value=True),
            reads=(src_key,), writes=(kB,))
        p.add("dve", lambda e, nb=nb, pr=pr: e.tensor_scalar(
            scS[0:pr, 0:nb], bmS[0:pr, 0:nb], 1.0 / 6.0, 1e-12, ALU.mult, ALU.max),
            reads=(kB,), writes=(kC,))
        p.add("dve", lambda e, nb=nb, pr=pr: e.reciprocal(
            rS[0:pr, 0:nb], scS[0:pr, 0:nb]),
            reads=(kC,), writes=(kR,))
        p.add("dve", lambda e, nb=nb, pr=pr: e.tensor_scalar_mul(
            shS[0:pr, 0:nb], scS[0:pr, 0:nb], 0.5),
            reads=(kC,), writes=(kH,))
        rbc = AP(rS, 0, [[QFD // 16, pr], [1, nb], [0, 16]])
        sbc = AP(shS, 0, [[QFD // 16, pr], [1, nb], [0, 16]])
        p.add("dve", lambda e, s=src_ap, r=rbc, fd=fd, pr=pr: e._custom_dve(
            T_ABS, out=tS[0:pr, 0:fd], in0=s, in1=r),
            reads=(src_key, kR), writes=(kT,))
        p.add("dve", lambda e, fd=fd, pr=pr: e._custom_dve(
            SNAP_LO, out=qloS[0:pr, 0:fd], in0=tS[0:pr, 0:fd], in1=c175[0:pr, 0:1],
            s0=0.25, s1=0.75, imm2=1.25),
            reads=(kT, "c175"), writes=(kQ,))
        p.add("dve", lambda e, fd=fd, pr=pr: e._custom_dve(
            SNAP_HI, out=q2S[0:pr, 0:fd], in0=tS[0:pr, 0:fd], in1=qloS[0:pr, 0:fd],
            s0=2.5, s1=3.5, imm2=5.0),
            reads=(kT, kQ), writes=(kT,))
        p.add(deq_eng, lambda e, s=sbc, fd=fd, pr=pr: e.tensor_tensor(
            dqS[0:pr, 0:fd], q2S[0:pr, 0:fd], s, ALU.mult),
            reads=(kT, kH), writes=(kQ,))
        p.add("dve", lambda e, o=out_ap, g=src_ap, fd=fd, pr=pr: e._custom_dve(
            FINAL_SIGN, out=o, in0=dqS[0:pr, 0:fd], in1=g),
            reads=(kQ, src_key), writes=(out_key,))

    def mm_group(emits, reads, writes, tag=""):
        def f(e, emits=tuple(emits)):
            last = None
            for fn in emits:
                last = fn(e)
            return last
        p.add("pe", f, reads=reads, writes=writes, tag=tag)

    def transpose8(src, src_key, dst, dst_key, bank0, n, f32mode):
        step = 4
        for c4 in range(0, n, step):
            nch = min(step, n - c4)
            bank = bank0 + ((c4 // step) % 2)
            for c in range(c4, c4 + nch):
                if f32mode:
                    p.add("pe", lambda e, c=c, c4=c4, bank=bank, src=src: e.transpose(
                        pb[bank][:, (c - c4) * 128:(c - c4 + 1) * 128],
                        src[:, c * 128:(c + 1) * 128], ident_f[:, :]),
                        reads=(src_key, "ident_f"), writes=(PS[bank],))
                else:
                    p.add("pe", lambda e, c=c, c4=c4, bank=bank, src=src: e.transpose(
                        pbb[bank][:, (c - c4) * 128:(c - c4 + 1) * 128],
                        src[:, c * 128:(c + 1) * 128], ident_b[:, :]),
                        reads=(src_key, "ident_b"), writes=(PS[bank],))
            if f32mode:
                p.add("act", lambda e, c4=c4, nch=nch, bank=bank, dst=dst: e.copy(
                    dst[:, c4 * 128:(c4 + nch) * 128], pb[bank][:, 0:nch * 128]),
                    reads=(PS[bank],), writes=(dst_key,))
            else:
                p.add("act", lambda e, c4=c4, nch=nch, bank=bank, dst=dst: e.copy(
                    dst[:, c4 * 128:(c4 + nch) * 128], pbb[bank][:, 0:nch * 128]),
                    reads=(PS[bank],), writes=(dst_key,))

    p.add("dve", lambda e: e.memset(c175[:, :], 1.75), writes=("c175",))
    p.add("dve", lambda e: e.memset(eps5[:, :], 1e-5), writes=("eps5",))
    p.add("dve", lambda e: e.memset(eps6[:, :], 1e-6), writes=("eps6",))
    p.add("dve", lambda e: e.memset(lnh[:, :], -0.6931471805599453), writes=("lnh",))
    p.add("dve", lambda e: e.memset(ones1[:, :], 1.0), writes=("ones1",))
    dma("sp", ident_f[:, :], idf_in[:, :], (), ("ident_f",))
    dma("sp", ident_b[:, :], idb_in[:, :], (), ("ident_b",))
    dma("sp", AP(msk_sb, 0, [[3 * 384, 128], [384, 3], [1, 384]]),
        AP(msk_in, 0, [[384, 128], [128 * 384, 3], [1, 384]]), (), ("msk_sb",))
    dma("sp", row_sb[0:1, 0:H], qg_in[0:1, :], (), ("row_q",))
    dma("sp", row_sb[0:1, 32:64], pm_in[0:1, :], (), ("row_pm",))
    p.add("act", lambda e: e.activation(row_sb[0:1, 16:32], row_sb[0:1, 0:H], AF.Abs),
          reads=("row_q",), writes=("row_a",))
    p.add("dve", lambda e: e.tensor_scalar_mul(
        row_sb[0:1, 16:32], row_sb[0:1, 16:32], -8.0),
        reads=("row_a",), writes=("row_a",))
    p.add("dve", lambda e: e.tensor_scalar_mul(
        row_sb[0:1, 0:16], row_sb[0:1, 0:16], 0.125),
        reads=("row_q",), writes=("row_q2",))
    mm_group([lambda e: e.matmul(pb[6][:, 0:64], ones1[0:1, :], row_sb[0:1, 0:64],
                                 start=True, stop=True)],
             reads=("ones1", "row_a", "row_q2", "row_pm"), writes=(PS[6],))
    p.add("act", lambda e: e.copy(gb_sb[:, 0:16], pb[6][:, 16:32]),
          reads=(PS[6],), writes=("gb_sb",))
    p.add("act", lambda e: e.copy(gb_sb[:, 16:32], pb[6][:, 0:16]),
          reads=(PS[6],), writes=("gb_sb",))
    p.add("act", lambda e: e.copy(pm_sb[:, :], pb[6][:, 32:64]),
          reads=(PS[6],), writes=("pm_sb",))

    for (wdram, wsb, wkey, ncols) in ((wq_t, wqT_sb, "wqT_sb", 1024),
                                      (wk_t, wkT_sb, "wkT_sb", 256),
                                      (wv_t, wvT_sb, "wvT_sb", 256),
                                      (wo_t, woT_sb, "woT_sb", 1024)):
        for c in range(8):
            st = stage[c % 2]
            dma("sp", st[:, 0:ncols], wdram[c * 128:(c + 1) * 128, :],
                (f"stg{c % 2}",), (f"stg{c % 2}",), tag="wcast_in")
            p.add("act", lambda e, st=st, wsb=wsb, c=c, ncols=ncols: e.copy(
                wsb[:, c * ncols:(c + 1) * ncols], st[:, 0:ncols]),
                reads=(f"stg{c % 2}",), writes=(wkey,))

    def quant_weight_full(dram_in, my_out, out_key):
        rows, cols = dram_in.shape
        for r in range(rows // 128):
            for h0 in range(0, cols, IHALF):
                fd = min(IHALF, cols - h0)
                if cols == D:
                    early = r < (rows // 128) // 2
                else:
                    early = h0 == 0
                dma("sp", wstg[:, 0:fd],
                    dram_in[r * 128:(r + 1) * 128, h0:h0 + fd],
                    ("wstg",), ("wstg",),
                    tag="wqheadE" if early else "wqheadL")
                quant_chain(wstg[:, 0:fd], wsq_sb[:, h0:h0 + fd],
                            fd, "wstg", "wsq_sb", deq_eng="pool", scr="w")
                for c4 in range(h0 // 128, (h0 + fd) // 128, 2):
                    nch = min(2, (h0 + fd) // 128 - c4)
                    bank = 2 + (c4 // 2) % 2
                    for c in range(c4, c4 + nch):
                        p.add("pe", lambda e, c=c, c4=c4, bank=bank: e.transpose(
                            pbb[bank][:, 768 + (c - c4) * 128:
                                      768 + (c - c4 + 1) * 128],
                            wsq_sb[:, c * 128:(c + 1) * 128], ident_b[:, :]),
                            reads=("wsq_sb", "ident_b"), writes=(PS[bank],))
                    reg = ((c4 // 2) % 4) * 256
                    p.add("act", lambda e, nch=nch, bank=bank, reg=reg: e.copy(
                        wtT_sb[:, reg:reg + nch * 128],
                        pbb[bank][:, 768:768 + nch * 128]),
                        reads=(PS[bank],), writes=(f"wtT:{(c4 // 2) % 4}",))
                    dma("pool",
                        AP(my_out, c4 * 128 * rows + r * 128,
                           [[rows, 128], [128 * rows, nch], [1, 128]]),
                        AP(wtT_sb, reg, [[1024, 128], [128, nch], [1, 128]]),
                        (f"wtT:{(c4 // 2) % 4}",), (f"{out_key}:{r}:{c4}",),
                        tag="wsh_out")

    wq_start = len(p.ops)
    quant_weight_full(wg_in, wgq_d, "wgq_d")
    quant_weight_full(wu_in, wuq_d, "wuq_d")
    quant_weight_full(wd_in, wdq_d, "wdq_d")
    wq_end = len(p.ops)
    def wgqkeys(hf):
        return tuple(f"wgq_d:{r}:{c4}" for r in range(hf * 11, hf * 11 + 11)
                     for c4 in range(0, D // 128, 2))

    def wuqkeys(hf):
        return tuple(f"wuq_d:{r}:{c4}" for r in range(hf * 11, hf * 11 + 11)
                     for c4 in range(0, D // 128, 2))

    def wdqkeys(hf):
        return tuple(f"wdq_d:{r}:{c4}" for r in range(D // 128)
                     for c4 in range(hf * 11, hf * 11 + 11, 2))

    def rope(src, src_key, dst, dst_key, nh, b3):
        def v(t, off):
            return AP(t, off, [[t.shape[1], 128], [64, nh], [1, 32]])

        def ctab(t):
            return AP(t, 0, [[32, 128], [0, nh], [1, 32]])
        s1a, s2a = v(src, 0), v(src, 32)
        d1a, d2a = v(dst, 0), v(dst, 32)
        cr, sr = ctab(cosr[b3]), ctab(sinr[b3])
        rt1v = AP(rt1, 0, [[512, 128], [32, nh], [1, 32]])
        rt2v = AP(rt2, 0, [[512, 128], [32, nh], [1, 32]])
        p.add("dve", lambda e: e.tensor_tensor(rt1v, s1a, cr, ALU.mult),
              reads=(src_key, f"cos{b3}"), writes=("rt1",))
        p.add("dve", lambda e: e.tensor_tensor(rt2v, s2a, sr, ALU.mult),
              reads=(src_key, f"sin{b3}"), writes=("rt2",))
        p.add("dve", lambda e: e.tensor_tensor(d1a, rt1v, rt2v, ALU.add),
              reads=("rt1", "rt2"), writes=(dst_key,))
        p.add("dve", lambda e: e.tensor_tensor(rt1v, s2a, cr, ALU.mult),
              reads=(src_key, f"cos{b3}"), writes=("rt1",))
        p.add("dve", lambda e: e.tensor_tensor(rt2v, s1a, sr, ALU.mult),
              reads=(src_key, f"sin{b3}"), writes=("rt2",))
        p.add("dve", lambda e: e.tensor_tensor(d2a, rt1v, rt2v, ALU.subtract),
              reads=("rt1", "rt2"), writes=(dst_key,))

    def p1_stage_a(it):
        bf = it % 2
        b3 = it % 3
        own = it >= 2
        xs = stage[bf]
        dma("sp", xs[:, :], x_in[it * 128:(it + 1) * 128, :],
            (f"stg{bf}",), (f"stg{bf}",), tag=f"x{it}")
        b4 = it % 4
        dma("sp", cosr[b4][:, :], cos_in[it * 128:(it + 1) * 128, :],
            (f"cos{b4}",), (f"cos{b4}",))
        dma("sp", sinr[b4][:, :], sin_in[it * 128:(it + 1) * 128, :],
            (f"sin{b4}",), (f"sin{b4}",))
        p.add("dve", lambda e: e.bn_stats(bn_s[:, 0:6], xs[:, 0:512]),
              reads=(f"stg{bf}",), writes=("bn_s",))
        p.add("dve", lambda e: e.bn_stats(bn_s[:, 6:12], xs[:, 512:1024]),
              reads=(f"stg{bf}",), writes=("bn_s",))
        p.add("dve", lambda e: e.bn_aggr(mv_s[:, :], bn_s[:, :]),
              reads=("bn_s",), writes=("mv_s",))
        p.add("act", lambda e: e.activation(r1_s[:, :], mv_s[:, 1:2], AF.Ln,
                                            bias=eps5[:, 0:1], scale=1.0),
              reads=("mv_s", "eps5"), writes=("r1_s",))
        p.add("act", lambda e: e.activation(r2_s[:, :], r1_s[:, :], AF.Exp,
                                            bias=0.0, scale=-0.5),
              reads=("r1_s",), writes=("r2_s",))
        p.add("dve", lambda e: e.tensor_scalar(nm_s[:, :], mv_s[:, 0:1],
                                               r2_s[:, 0:1], -1.0,
                                               ALU.mult, ALU.mult),
              reads=("mv_s", "r2_s"), writes=("nm_s",))
        p.add("act", lambda e: e.activation(xn_t[bf][:, :], xs[:, :], AF.Identity,
                                            bias=nm_s[:, 0:1], scale=r2_s[:, 0:1]),
              reads=(f"stg{bf}", "nm_s", "r2_s"), writes=(f"xn{bf}",))
        transpose8(xn_t[bf], f"xn{bf}", xnT_t[b3], f"xnT{b3}", 0, 8, True)
        if own:
            quant_chain(xn_t[bf][:, :], xq_t[bf][:, :], 1024, f"xn{bf}",
                        f"xq{bf}", deq_eng="dve")
            if _DEBUG:
                dma("pool", xq_dbg[(it - 2) * 128:(it - 1) * 128, :],
                    xq_t[bf][:, :], (f"xq{bf}",), (f"xqdbg:{it}",))

    def p1_stage_a2(it):
        bf = it % 2
        transpose8(xq_t[bf], f"xq{bf}", xqT_t[bf], f"xqT{bf}", 2, 8, False)
        dma("pool",
            AP(xqT_d, (it - 2) * 128, [[TO, 128], [128 * TO, 8], [1, 128]]),
            AP(xqT_t[bf], 0, [[1024, 128], [128, 8], [1, 128]]),
            (f"xqT{bf}",), (f"xqT_d:{it - 2}",), tag="xqT_out")

    def p1_stage_b1(it):
        bf = it % 2
        b3 = it % 3
        own = it >= 2

        def kv_mms(wsb, wkey, bank):
            ems = []
            for c in range(8):
                ems.append(lambda e, c=c, wsb=wsb, bank=bank: e.matmul(
                    pb[bank][:, 0:256], xnT_t[b3][:, c * 128:(c + 1) * 128],
                    wsb[:, c * 256:(c + 1) * 256],
                    start=(c == 0), stop=(c == 7)))
            mm_group(ems, reads=(f"xnT{b3}", wkey), writes=(PS[bank],), tag="kvmm")

        kv_mms(wkT_sb, "wkT_sb", 6)
        if own:
            for half in range(2):
                ems = []
                for c in range(8):
                    ems.append(lambda e, c=c, half=half: e.matmul(
                        pb[4 + half][:, 0:512],
                        xnT_t[b3][:, c * 128:(c + 1) * 128],
                        wqT_sb[:, c * 1024 + half * 512: c * 1024 + half * 512 + 512],
                        start=(c == 0), stop=(c == 7)))
                mm_group(ems, reads=(f"xnT{b3}", "wqT_sb"),
                         writes=(PS[4 + half],), tag="qmm")
        p.add("act", lambda e, bf=bf: e.activation(
            sq_t[bf][:, 1024:1280], AP(pb[6], 0, [[512, 128], [1, 256]]),
            AF.Square), reads=(PS[6],), writes=(f"sq{bf}",))
        if own:
            for half in range(2):
                p.add("act", lambda e, half=half, bf=bf: e.activation(
                    sq_t[bf][:, half * 512:(half + 1) * 512],
                    AP(pb[4 + half], 0, [[512, 128], [1, 512]]), AF.Square),
                    reads=(PS[4 + half],), writes=(f"sq{bf}",))
        p.add("act", lambda e, bf=bf: e.copy(
            kn_t[bf][:, :], AP(pb[6], 0, [[512, 128], [1, 256]])),
            reads=(PS[6],), writes=(f"kn{bf}",))
        if own:
            for half in range(2):
                p.add("act", lambda e, half=half, bf=bf: e.copy(
                    AP(qn_t[bf], half * 512, [[1024, 128], [1, 512]]),
                    AP(pb[4 + half], 0, [[512, 128], [1, 512]])),
                    reads=(PS[4 + half],), writes=(f"qn{bf}",))
        kv_mms(wvT_sb, "wvT_sb", 7)
        p.add("act", lambda e: e.copy(
            AP(v_t[bf], 0, [[260, 128], [65, 4], [1, 64]]),
            AP(pb[7], 0, [[512, 128], [64, 4], [1, 64]])),
            reads=(PS[7], f"v{bf}"), writes=(f"v{bf}",))
        dma("pool", v_d[it, :, :], v_t[bf][:, :], (f"v{bf}",), (f"v_d:{it}",))

    def p1_stage_b2_head(it):
        bf = it % 2
        own = it >= 2
        nh_tot = 20 if own else 4
        sq3 = AP(sq_t[bf], 1024 if not own else 0,
                 [[1280, 128], [64, nh_tot], [1, 64]])
        p.add("dve", lambda e, sq3=sq3, nh=nh_tot: e.tensor_reduce(
            rk_s[:, 0:nh], sq3, AX.X, ALU.add),
            reads=(f"sq{bf}",), writes=("rk_s",))
        p.add("act", lambda e, nh=nh_tot: e.activation(
            rk2_s[:, 0:nh], rk_s[:, 0:nh], AF.Ln, bias=eps6[:, 0:1], scale=1.0 / 64),
            reads=("rk_s", "eps6"), writes=("rk2_s",))
        p.add("act", lambda e, nh=nh_tot: e.activation(
            rk_s[:, 0:nh], rk2_s[:, 0:nh], AF.Exp, bias=0.0, scale=-0.5),
            reads=("rk2_s",), writes=("rk_s",))

    def p1_stage_b2(it):
        bf = it % 2
        b4 = it % 4
        own = it >= 2
        koff = 16 if own else 0
        rkbc = AP(rk_s, koff, [[20, 128], [1, 4], [0, 64]])
        p.add("dve", lambda e, bf=bf, rkbc=rkbc: e.tensor_tensor(
            kn_t[bf][:, :], kn_t[bf][:, :], rkbc, ALU.mult),
            reads=(f"kn{bf}", "rk_s"), writes=(f"kn{bf}",))
        rope(kn_t[bf], f"kn{bf}", kr_t[bf], f"kr{bf}", 4, b4)
        p.add("dve", lambda e: e.tensor_copy(
            AP(krd_t[bf], 0, [[512, 128], [128, 4], [64, 2], [1, 64]]),
            AP(kr_t[bf], 0, [[256, 128], [64, 4], [0, 2], [1, 64]])),
            reads=(f"kr{bf}",), writes=(f"krd{bf}",))
        if own:
            for half in range(2):
                rbc = AP(rk_s, half * 8, [[20, 128], [1, 8], [0, 64]])
                p.add("dve", lambda e, half=half, bf=bf, rbc=rbc: e.tensor_tensor(
                    AP(qn_t[bf], half * 512, [[1024, 128], [1, 512]]),
                    AP(qn_t[bf], half * 512, [[1024, 128], [1, 512]]),
                    rbc, ALU.mult),
                    reads=(f"qn{bf}", "rk_s"), writes=(f"qn{bf}",))
            rope(qn_t[bf], f"qn{bf}", qr_t[bf], f"qr{bf}", 16, b4)

    def p1_stage_b3(it):
        bf = it % 2
        own = it >= 2
        for c in range(4):
            p.add("pe", lambda e, c=c: e.transpose(
                pbb[3][:, 0:128], krd_t[bf][:, c * 128:(c + 1) * 128],
                ident_b[:, :]),
                reads=(f"krd{bf}", "ident_b"), writes=(PS[3],))
            p.add("act", lambda e, c=c: e.copy(
                kT_t[bf][:, c * 128:(c + 1) * 128], pbb[3][:, 0:128]),
                reads=(PS[3],), writes=(f"kT{bf}",))
        dma("pool",
            AP(kT_d, it * 128, [[TT, 128], [128 * TT, 4], [1, 128]]),
            AP(kT_t[bf], 0, [[512, 128], [128, 4], [1, 128]]),
            (f"kT{bf}",), (f"kT_d:{it}",))
        if own:
            transpose8(qr_t[bf], f"qr{bf}", qT_t[bf], f"qT{bf}", 0, 8, False)
            dma("pool",
                AP(qT_d, (it - 2) * 128, [[TO, 128], [128 * TO, 8], [1, 128]]),
                AP(qT_t[bf], 0, [[1024, 128], [128, 8], [1, 128]]),
                (f"qT{bf}",), (f"qT_d:{it - 2}",), tag="qT_out")

    def p2_stage_a(qi):
        bf = qi % 2
        it = qi + 2
        dma("sp", AP(qt_sb[bf], 0, [[1024, 128], [128, 8], [1, 128]]),
            AP(qT_d, qi * 128, [[TO, 128], [128 * TO, 8], [1, 128]]),
            (f"qT_d:{qi}", f"qts{bf}"), (f"qts{bf}",), tag="qt_in")
        dma("sp", AP(kt_sb[bf], 0, [[1536, 128], [384, 4], [1, 384]]),
            AP(kT_d, qi * 128, [[TT, 128], [128 * TT, 4], [1, 384]]),
            tuple([f"kT_d:{j}" for j in range(qi, qi + 3)]) + (f"kts{bf}",),
            (f"kts{bf}",), tag="kt_in")
        dma("sp", AP(v2_sb[bf], 0, [[780, 128], [260, 3], [1, 260]]),
            AP(v_d, qi * 128 * 260, [[260, 128], [128 * 260, 3], [1, 260]]),
            tuple([f"v_d:{j}" for j in range(qi, qi + 3)]) + (f"v2s{bf}",),
            (f"v2s{bf}",), tag="v_in")
        dma("sp", xres_sb[bf][:, :], x_in[it * 128:(it + 1) * 128, :],
            (f"xres{bf}",), (f"xres{bf}",), tag="xres")
        if _DEBUG and qi == 1:
            dma("pool", qk_dbg[0, :, 0:1024], qt_sb[bf][:, :], (f"qts{bf}",), ("qkdbg0",))
            dma("pool", qk_dbg[1, :, :], kt_sb[bf][:, :], (f"kts{bf}",), ("qkdbg1",))

    def p2_scores(qi, g):
        bf = qi % 2
        sel = min(qi, 2)
        if True:
            for k2 in range(2):
                m = 2 * g + k2
                a, b2 = PAIRS[m]
                sb0 = (m % 2) * 2
                kv = a // 4
                for c in range(3):
                    def em(e, c=c, m=m, sb0=sb0, kv=kv, bf=bf):
                        e.matmul(pb[sb0][:, c * 128:(c + 1) * 128],
                                 kt_sb[bf][0:64, kv * 384 + c * 128:
                                           kv * 384 + (c + 1) * 128],
                                 qt_sb[bf][0:64, m * 128:(m + 1) * 128],
                                 start=True, stop=True, tile_position=(0, 0))
                        return e.matmul(
                            pb[sb0 + 1][:, c * 128:(c + 1) * 128],
                            kt_sb[bf][64:128, kv * 384 + c * 128:
                                      kv * 384 + (c + 1) * 128],
                            qt_sb[bf][64:128, m * 128:(m + 1) * 128],
                            start=True, stop=True, tile_position=(64, 0))
                    p.add("pe", em, reads=(f"kts{bf}", f"qts{bf}"),
                          writes=(PS[sb0], PS[sb0 + 1]), tag="score")
                for j, hh in enumerate((a, b2)):
                    s = 2 * k2 + j
                    p.add("act", lambda e, s=s, hh=hh, bk=sb0 + j, bf=bf, g=g:
                          e.activation(
                              e_sb[bf][g % 2][:, s * 384:(s + 1) * 384],
                              AP(pb[bk], 0, [[512, 128], [1, 384]]),
                              AF.Exp, bias=gb_sb[:, hh:hh + 1],
                              scale=gb_sb[:, 16 + hh:17 + hh]),
                          reads=(PS[sb0 + j], "gb_sb"), writes=(f"e{bf}{g % 2}",))
            p.add("dve", lambda e, sel=sel, bf=bf, g=g: e.tensor_tensor(
                AP(e_sb[bf][g % 2], 0, [[1536, 128], [384, 4], [1, 384]]),
                AP(e_sb[bf][g % 2], 0, [[1536, 128], [384, 4], [1, 384]]),
                AP(msk_sb, sel * 384, [[3 * 384, 128], [0, 4], [1, 384]]),
                ALU.mult),
                reads=(f"e{bf}{g % 2}", "msk_sb"), writes=(f"e{bf}{g % 2}",),
                tag="mask")
            if _DEBUG and qi == 1:
                dma("pool", e2_dbg[g, :, :], e_sb[bf][g % 2][:, :],
                    (f"e{bf}{g % 2}",), (f"e2dbg:{g}",), tag="e2dbg")

    def p2_y(qi, g):
        bf = qi % 2
        ybank = 4 + (g % 2)
        if True:
            for k2 in range(2):
                m = 2 * g + k2
                a, b2 = PAIRS[m]
                for j, hh in enumerate((a, b2)):
                    s = 2 * k2 + j
                    kv = hh // 4
                    ems = []
                    for c in range(3):
                        ems.append(lambda e, c=c, s=s, kv=kv, yb=ybank, bf=bf,
                                   g=g: e.matmul(
                                       pb[yb][:, s * 65:(s + 1) * 65],
                                       e_sb[bf][g % 2][:, s * 384 + c * 128:
                                                       s * 384 + (c + 1) * 128],
                                       v2_sb[bf][:, c * 260 + kv * 65:
                                                 c * 260 + (kv + 1) * 65],
                                       start=(c == 0), stop=(c == 2)))
                    mm_group(ems, reads=(f"e{bf}{g % 2}", f"v2s{bf}"),
                             writes=(PS[ybank],), tag="ymm")
            p.add("act", lambda e, g=g, yb=ybank, bf=bf: e.copy(
                AP(y65_sb[bf], 4 * g * 65, [[1040, 128], [1, 260]]),
                AP(pb[yb], 0, [[512, 128], [1, 260]])),
                reads=(PS[ybank],), writes=(f"y65{bf}",))

    def p2_stage_c(qi):
        bf = qi % 2
        p.add("dve", lambda e, bf=bf: e.reciprocal(
            rp_sb[:, :], AP(y65_sb[bf], 64, [[1040, 128], [65, 16]])),
            reads=(f"y65{bf}",), writes=("rp_sb",))
        p.add("dve", lambda e: e.tensor_tensor(
            sall_sb[:, :], AP(rp_sb, 0, [[16, 128], [2, 8], [0, 2], [1, 2]]),
            pm_sb[:, :], ALU.mult),
            reads=("rp_sb", "pm_sb"), writes=("sall_sb",))
        for pp in range(8):
            for o in range(2):
                p.add("dve", lambda e, pp=pp, o=o, bf=bf: e._custom_dve(
                    PMIX,
                    out=ymix_sb[bf][:, (2 * pp + o) * 64:(2 * pp + o + 1) * 64],
                    in0=AP(y65_sb[bf], (2 * pp) * 65, [[1040, 128], [1, 64]]),
                    in1=AP(y65_sb[bf], (2 * pp + 1) * 65, [[1040, 128], [1, 64]]),
                    s0=sall_sb[:, pp * 4 + o * 2: pp * 4 + o * 2 + 1],
                    s1=sall_sb[:, pp * 4 + o * 2 + 1: pp * 4 + o * 2 + 2]),
                    reads=(f"y65{bf}", "sall_sb"), writes=(f"ymix{bf}",),
                    tag="pmix")
        if _DEBUG:
            dma("pool", ym_dbg[qi * 128:(qi + 1) * 128, :], ymix_sb[bf][:, :],
                (f"ymix{bf}",), (f"ymdbg:{qi}",))

    def p2_stage_c1b(qi):
        bf = qi % 2
        transpose8(ymix_sb[bf], f"ymix{bf}", ymixT_sb[bf], f"ymixT{bf}", 0, 8, False)
        for half in range(2):
            ems = []
            for c in range(8):
                ems.append(lambda e, c=c, half=half, bf=bf: e.matmul(
                    pb[6 + half][:, 0:512],
                    ymixT_sb[bf][:, c * 128:(c + 1) * 128],
                    woT_sb[:, c * 1024 + half * 512: c * 1024 + half * 512 + 512],
                    start=(c == 0), stop=(c == 7)))
            mm_group(ems, reads=(f"ymixT{bf}", "woT_sb"),
                     writes=(PS[6 + half],), tag="omm")

    def p2_stage_c2(qi):
        bf = qi % 2
        for half in range(2):
            p.add("dve", lambda e, half=half, bf=bf: e.tensor_tensor(
                ostage[bf][:, half * 512:(half + 1) * 512],
                AP(pb[6 + half], 0, [[512, 128], [1, 512]]),
                xres_sb[bf][:, half * 512:(half + 1) * 512], ALU.add),
                reads=(PS[6 + half], f"xres{bf}"), writes=(f"ost{bf}",))
        dma("pool", acc_d[qi * 128:(qi + 1) * 128, :], ostage[bf][:, :],
            (f"ost{bf}",), (f"acc_d:{qi}",), tag="acc_out")

    def emit_p2_tile(qi, pending_c, pre=0):
        if pre == 0:
            p2_scores(qi, 0)
        for g in range(4):
            if g + 1 < 4 and g + 1 >= pre:
                p2_scores(qi, g + 1)
            if g == 2 and pending_c:
                p2_stage_c1b(pending_c[-1])
            p2_y(qi, g)
            if g == 0 and pending_c:
                p2_stage_c(pending_c[-1])
        if pending_c:
            p2_stage_c2(pending_c.pop())
        if qi + 1 < NQ:
            p2_stage_a(qi + 1)
        pending_c.append(qi)
        p.add("sp", lambda e: e.nop(), tag="anchor3")

    pending_c = []
    for i in range(2):
        p.add("dve", lambda e, i=i: e.memset(v_t[i][:, :], 1.0),
              reads=(), writes=(f"v{i}",))
    p1_stage_a(0)
    p1_stage_a(1)
    for it in range(NT):
        if it >= 2:
            p1_stage_b3(it - 2)
        if 2 <= it + 1 < NT:
            p1_stage_a2(it + 1)
        if it + 2 < NT:
            p1_stage_a(it + 2)
        if it >= 1:
            p1_stage_b2_head(it - 1)
        qi = it - 5
        if qi == 0:
            p2_stage_a(0)
        pre = 2 if 0 <= qi < NQ else 0
        if pre:
            p2_scores(qi, 0)
            p2_scores(qi, 1)
        p1_stage_b1(it)
        if it >= 1:
            p1_stage_b2(it - 1)
        p.add("sp", lambda e: e.nop(), tag="anchor1")
        if 0 <= qi < NQ:
            emit_p2_tile(qi, pending_c, pre)
    p1_stage_b2_head(NT - 1)
    p1_stage_b2(NT - 1)
    p1_stage_b3(NT - 2)
    p1_stage_b3(NT - 1)

    for qi in range(NT - 5, NQ):
        emit_p2_tile(qi, pending_c)
    last = pending_c.pop()
    p2_stage_c(last)
    p2_stage_c1b(last)
    p2_stage_c2(last)

    p.add("sp", lambda e: e.nop(), reads=tuple(arA.keys), writes=("arena_free",),
          tag="barrier")

    def build_ffn_weights(hf):
        for c in range(8):
            dma("sp", wg_sb[:, c * IHALF:(c + 1) * IHALF],
                wgq_d[c * 128:(c + 1) * 128, hf * IHALF:(hf + 1) * IHALF],
                wgqkeys(hf) + ("arena_free", f"wg:{c}"), (f"wg:{c}",),
                tag="wgin" if hf == 0 else "wgin2")
            dma("sp", wu_sb[:, c * IHALF:(c + 1) * IHALF],
                wuq_d[c * 128:(c + 1) * 128, hf * IHALF:(hf + 1) * IHALF],
                wuqkeys(hf) + ("arena_free", f"wu:{c}"), (f"wu:{c}",),
                tag="wuin")
        for ch in range(NCH):
            dma("sp", wd_sb[:, ch * 1024:(ch + 1) * 1024],
                wdq_d[hf * IHALF + ch * 128: hf * IHALF + (ch + 1) * 128, :],
                wdqkeys(hf) + ("arena_free", f"wd:{ch}"), (f"wd:{ch}",),
                tag="wdin")

    WGKEYS = tuple(f"wg:{c}" for c in range(8))
    WUKEYS = tuple(f"wu:{c}" for c in range(8))
    WDKEYS = tuple(f"wd:{ch}" for ch in range(NCH))
    REG = ((0, 0, 512), (1, 512, 1024), (2, 1024, 1408))

    def ffn_stage_a(hf, qi):
        bf = qi % 2
        hb = qi % 4
        dma("sp", AP(xqs_sb[bf], 0, [[1024, 128], [128, 8], [1, 128]]),
            AP(xqT_d, qi * 128, [[TO, 128], [128 * TO, 8], [1, 128]]),
            (f"xqT_d:{qi}", "arena_free", f"xqs{bf}"), (f"xqs{bf}",), tag="xq_in")
        for (wsb, wkeys, b0) in ((wg_sb, WGKEYS, 0), (wu_sb, WUKEYS, 3)):
            for (ri, c0, c1) in REG:
                ems = []
                for c in range(8):
                    ems.append(lambda e, c=c, c0=c0, c1=c1, ri=ri, wsb=wsb,
                               b0=b0, bf=bf: e.matmul(
                        pb[b0 + ri][:, 0:c1 - c0],
                        xqs_sb[bf][:, c * 128:(c + 1) * 128],
                        wsb[:, c * IHALF + c0: c * IHALF + c1],
                        start=(c == 0), stop=(c == 7)))
                mm_group(ems, reads=(f"xqs{bf}",) + wkeys,
                         writes=(PS[b0 + ri],), tag="gumm")
        for (ri, c0, c1) in REG:
            p.add("act", lambda e, ri=ri, c0=c0, c1=c1, bf=bf: e.activation(
                sg_sb[bf][:, c0:c1], AP(pb[ri], 0, [[512, 128], [1, c1 - c0]]),
                AF.Silu),
                reads=(PS[ri],), writes=(f"sg_sb{bf}",))

    def ffn_stage_h(hf, qi):
        bf = qi % 2
        hb = qi % 4
        for (ri, c0, c1) in REG:
            p.add("dve", lambda e, ri=ri, c0=c0, c1=c1, bf=bf, hb=hb:
                  e.tensor_tensor(
                      h_sb[hb][:, c0:c1], sg_sb[bf][:, c0:c1],
                      AP(pb[3 + ri], 0, [[512, 128], [1, c1 - c0]]), ALU.mult),
                  reads=(f"sg_sb{bf}", PS[3 + ri]), writes=(f"h_sb{hb}",))

    def ffn_stage_b(hf, qi):
        bf = qi % 2
        hb = qi % 4
        accsrc = acc_d if hf == 0 else acc2_d
        acckey = "acc_d" if hf == 0 else "acc2_d"
        dma("sp", ain_sb[bf][:, :], accsrc[qi * 128:(qi + 1) * 128, :],
            (f"{acckey}:{qi}", "arena_free", f"ain{bf}"), (f"ain{bf}",),
            tag="acc_in")
        quant_chain(h_sb[hb][:, 0:704], hq_sb[bf][:, 0:704], 704, f"h_sb{hb}",
                    f"hq_sb{bf}")
        quant_chain(h_sb[hb][:, 704:IHALF], hq_sb[bf][:, 704:IHALF], 704,
                    f"h_sb{hb}", f"hq_sb{bf}")
        transpose8(hq_sb[bf], f"hq_sb{bf}", hqT_sb[bf], f"hqT_sb{bf}", 6, NCH,
                   False)
        for half in range(2):
            ems = []
            for c in range(NCH):
                ems.append(lambda e, c=c, half=half, bf=bf: e.matmul(
                    pb[6 + half][:, 0:512],
                    hqT_sb[bf][:, c * 128:(c + 1) * 128],
                    wd_sb[:, c * 1024 + half * 512: c * 1024 + half * 512 + 512],
                    start=(c == 0), stop=(c == NCH - 1)))
            mm_group(ems, reads=(f"hqT_sb{bf}",) + WDKEYS,
                     writes=(PS[6 + half],), tag="dmm")
            p.add("dve", lambda e, half=half, bf=bf: e.tensor_tensor(
                ostage[bf][:, half * 512:(half + 1) * 512],
                AP(pb[6 + half], 0, [[512, 128], [1, 512]]),
                ain_sb[bf][:, half * 512:(half + 1) * 512], ALU.add),
                reads=(PS[6 + half], f"ain{bf}"), writes=(f"ost{bf}",))
        dst = acc2_d if hf == 0 else y_out
        dkey = f"acc2_d:{qi}" if hf == 0 else f"out:{qi}"
        dma("pool", dst[qi * 128:(qi + 1) * 128, :], ostage[bf][:, :],
            (f"ost{bf}",), (dkey,), tag="final_out")

    for hf in range(2):
        build_ffn_weights(hf)
        for j in range(2):
            ffn_stage_a(hf, j)
            ffn_stage_h(hf, j)
        for qi in range(NQ):
            if qi + 2 < NQ:
                ffn_stage_a(hf, qi + 2)
            ffn_stage_b(hf, qi)
            if qi + 2 < NQ:
                ffn_stage_h(hf, qi + 2)

    p.add("sp", lambda e: e.nop(), reads=tuple(f"out:{qi}" for qi in range(NQ)))

    wq_ops = p.ops[wq_start:wq_end]
    rest = p.ops[:wq_start] + p.ops[wq_end:]
    heads = [i for i, op in enumerate(wq_ops)
             if op["tag"] in ("wqheadE", "wqheadL")]
    heads.append(len(wq_ops))
    chunks = [wq_ops[heads[i]:heads[i + 1]] for i in range(len(heads) - 1)]
    early_chunks = [ch for ch in chunks if ch[0]["tag"] == "wqheadE"]
    late_chunks = [ch for ch in chunks if ch[0]["tag"] == "wqheadL"]
    FINE = True
    flat = [op for ch in chunks for op in ch]
    if not FINE:
        out, ci = [], 0
        last_anchor = max(i for i, op in enumerate(rest) if op["tag"] in
                          ("anchor1", "anchor3"))
        for i, op in enumerate(rest):
            out.append(op)
            if op["tag"] in ("anchor1", "anchor3") and ci < len(chunks):
                per = 1 if op["tag"] == "anchor1" else 2
                if i == last_anchor:
                    per = len(chunks) - ci
                for ch in chunks[ci:ci + per]:
                    out.extend(ch)
                ci += per
        fin = out.pop(next(i for i, op in enumerate(out) if op is p.ops[-1]))
        out.append(fin)
        p.ops = out
        return _finish(nc, p)
    n_dve = sum(1 for op in rest if op["e"] == "dve"
                and op["tag"] not in ("wqhead",))
    first_b = next(i for i, op in enumerate(rest) if op["tag"] == "wgin")
    n_elig = sum(1 for op in rest[:first_b] if op["e"] == "dve")
    rate = len(flat) / max(1, n_elig)
    out, wi, acc = [], 0, 0.0
    for i, op in enumerate(rest):
        if i == first_b:
            out.extend(flat[wi:])
            wi = len(flat)
        if i < first_b and op["e"] == "dve" and wi < len(flat):
            acc += rate
            while acc >= 1.0 and wi < len(flat):
                out.append(flat[wi])
                wi += 1
                acc -= 1.0
        out.append(op)
    if wi < len(flat):
        out.extend(flat[wi:])
    fin = out.pop(next(i for i, op in enumerate(out) if op is p.ops[-1]))
    out.append(fin)
    p.ops = out
    return _finish(nc, p)


def _finish(nc, p):
    p.schedule()
    es = contextlib.ExitStack()
    sems = {}
    for e in Prog.ENGINES:
        sems[("c", e)] = es.enter_context(nc.semaphore(f"cs_{e}"))
    for qn in Prog.QUEUES:
        sems[("d", qn)] = es.enter_context(nc.semaphore(f"ds_{qn}"))
    block = es.enter_context(nc.Block())

    def emit_engine(eng_obj, ename):
        for op in p.ops:
            if op["e"] != ename:
                continue
            for (key, val) in op["waits"]:
                eng_obj.wait_ge(sems[key], val)
            inst = op["f"](eng_obj)
            if op["inc"]:
                if op["dma"]:
                    inst.then_inc(sems[("d", op["q"])], 16)
                else:
                    inst.then_inc(sems[("c", ename)], 1)

    block.tensor(lambda e: emit_engine(e, "pe"))
    block.vector(lambda e: emit_engine(e, "dve"))
    block.scalar(lambda e: emit_engine(e, "act"))
    block.gpsimd(lambda e: emit_engine(e, "pool"))
    block.sync(lambda e: emit_engine(e, "sp"))
    es.close()
    mybir.codegen_inst_isa_subclasses(nc)
    return nc


_DEBUG = False
_NC_CACHE = None


def _get_nc():
    global _NC_CACHE
    if _NC_CACHE is None:
        _NC_CACHE = _build_program()
    return _NC_CACHE


def _rope_tables():
    inv = 1.0 / (10000.0 ** (np.arange(0, HD, 2, dtype=np.float32) / HD))
    t = np.arange(S, dtype=np.float32)
    fr = np.outer(t, inv)
    return np.cos(fr).astype(np.float32), np.sin(fr).astype(np.float32)


def _masks(seqstart):
    r = np.arange(128)
    m0 = (r[None, :] < r[:, None]).astype(np.float32)
    m2 = (r[None, :] >= r[:, None]).astype(np.float32)
    ones = np.ones((128, 128), np.float32)
    zeros = np.zeros((128, 128), np.float32)
    normal = np.concatenate([m0, ones, m2], axis=1)
    if not seqstart:
        out = np.stack([normal, normal, normal])
    else:
        t0 = np.concatenate([zeros, zeros, m2], axis=1)
        t1 = np.concatenate([zeros, ones, m2], axis=1)
        out = np.stack([t0, t1, normal])
    return out.astype(ml_dtypes.bfloat16)


def kernel(L0_attn_x, L0_attn_Wq, L0_attn_Wk, L0_attn_Wv, L0_attn_Wo,
           L0_attn_q_gain, L0_attn_pair_mix, L0_ffn_w_gate, L0_ffn_w_up,
           L0_ffn_w_down):
    x = np.ascontiguousarray(np.asarray(L0_attn_x, np.float32))
    cos_f, sin_f = _rope_tables()
    nc = _get_nc()
    in_maps = []
    for c in range(NCORE):
        b, q = c // 4, c % 4
        own = x[b, q * TO:(q + 1) * TO]
        halo = (np.zeros((HALO, D), np.float32) if q == 0
                else x[b, q * TO - HALO: q * TO])
        p0 = q * TO - HALO
        pos = np.clip(np.arange(p0, p0 + TT), 0, S - 1)
        m = dict(
            x_in=np.concatenate([halo, own], 0),
            wq_t=np.ascontiguousarray(np.asarray(L0_attn_Wq, np.float32).T),
            wk_t=np.ascontiguousarray(np.asarray(L0_attn_Wk, np.float32).T),
            wv_t=np.ascontiguousarray(np.asarray(L0_attn_Wv, np.float32).T),
            wo_t=np.ascontiguousarray(np.asarray(L0_attn_Wo, np.float32).T),
            wg_in=np.ascontiguousarray(np.asarray(L0_ffn_w_gate, np.float32)),
            wu_in=np.ascontiguousarray(np.asarray(L0_ffn_w_up, np.float32)),
            wd_in=np.ascontiguousarray(np.asarray(L0_ffn_w_down, np.float32)),
            qg_in=np.asarray(L0_attn_q_gain, np.float32).reshape(1, H),
            pm_in=np.asarray(L0_attn_pair_mix, np.float32).reshape(1, 32),
            cos_in=np.ascontiguousarray(cos_f[pos]),
            sin_in=np.ascontiguousarray(sin_f[pos]),
            msk_in=_masks(q == 0),
            idf_in=np.eye(128, dtype=np.float32),
            idb_in=np.eye(128).astype(ml_dtypes.bfloat16),
        )
        in_maps.append(m)
    res = run_bass_kernel_spmd(nc, in_maps, core_ids=list(range(NCORE)))
    out = np.empty((Bb, S, D), np.float32)
    for c in range(NCORE):
        b, q = c // 4, c % 4
        out[b, q * TO:(q + 1) * TO] = res.results[c]["y_out"]
    return out


# version 33 (final)
# speedup vs baseline: 1.0386x; 1.0386x over previous
import contextlib

import numpy as np
import ml_dtypes

import concourse.bass as bass
import concourse.mybir as mybir
from concourse.ap import AP
from concourse.bass_utils import run_bass_kernel_spmd
import concourse.dve_ops as dve_ops
from concourse.dve_op import DveOp
from concourse.dve_spec import Spec, Src0, Src1, C0, C1, C2, C3, Zero, One, lower, maxx

F32 = mybir.dt.float32
BF16 = mybir.dt.bfloat16
U8 = mybir.dt.uint8
AF = mybir.ActivationFunctionType
ALU = mybir.AluOpType
AX = mybir.AxisListType

NCORE = 8
Bb, S, D = 2, 8192, 1024
H, KVH, HD = 16, 4, 64
WIN = 256
II = 2816
TO = 2048
HALO = 256
TT = TO + HALO
NT = TT // 128
NQ = TO // 128
IHALF = II // 2
NCH = IHALF // 128
WG_SH = II // NCORE
WD_SH = D // NCORE
PAIRS = [(2 * i, 2 * i + 1) for i in range(8)]

_REG = {}


def _mk_op(name, spec, reference, steal_from):
    if name in _REG:
        return _REG[name]
    shas = {}
    for v in ("v3", "v4"):
        try:
            shas[v] = lower(spec, ver=v, perf_en=False, name=name).sha(v)
        except Exception:
            pass
    op = DveOp(name, spec, subdim=False, uops_sha=shas, reference=reference)
    dve_ops.OPS.append(op)
    dve_ops._SUB_OPCODE_FOR_NAME[name] = dve_ops._SUB_OPCODE_FOR_NAME[steal_from]
    dve_ops.CUSTOM_DVE_SPECS[name] = spec
    dve_ops.CUSTOM_DVE_REFS[name] = reference
    _REG[name] = op
    return op


def _f32(x):
    return np.asarray(x, np.float32)


T_ABS = _mk_op(
    "ANT_T_ABS", Spec(body=maxx(Src0 * Src1, Zero - Src0 * Src1)),
    lambda in0, in1, s0, s1, imm2: _f32(np.abs(_f32(in0) * _f32(in1))),
    "BITWISE_XOR_TT_BV")
SNAP_LO = _mk_op(
    "ANT_SNAP_LO", Spec(body=(Src0 > C0) + (Src0 > C1) + (Src0 > C2) + (Src0 > C3)),
    lambda in0, in1, s0, s1, imm2: _f32(
        _f32(in0 > s0) + _f32(in0 > s1) + _f32(in0 > imm2) + _f32(in0 > in1)),
    "DVEHASH32L_SEED")
_c5 = (Src0 > C0); _c6 = (Src0 > C1); _c7 = (Src0 > C2)
_s1 = _c5 + _c6; _s2 = _s1 + _c7; _s3 = _s2 + _c7; _s4 = _s3 + _s3
SNAP_HI = _mk_op(
    "ANT_SNAP_HI", Spec(body=_s4 + Src1),
    lambda in0, in1, s0, s1, imm2: _f32(
        2.0 * (_f32(in0 > s0) + _f32(in0 > s1) + 2.0 * _f32(in0 > imm2)) + _f32(in1)),
    "DVEHASH32L_BODY")
FINAL_SIGN = _mk_op(
    "ANT_FINAL_SIGN2", Spec(body=Src0 * ((Src1 >= Zero) + (Src1 >= Zero) - One)),
    lambda in0, in1, s0, s1, imm2: _f32(
        _f32(in0) * (2.0 * _f32(in1 >= 0) - 1.0)),
    "DVEHASH32L_DRAIN")
PMIX = _mk_op(
    "ANT_PMIX", Spec(body=Src0 * C0 + Src1 * C1),
    lambda in0, in1, s0, s1, imm2: _f32(_f32(in0) * _f32(s0) + _f32(in1) * _f32(s1)),
    "CODY_WAITE_CASCADE")


class Prog:
    ENGINES = ("pe", "dve", "act", "pool", "sp")
    QUEUES = ("pe", "dve", "act", "pool", "sp", "cc")

    def __init__(self):
        self.ops = []

    def add(self, engine, emit, reads=(), writes=(), dma=False, q=None, tag=""):
        self.ops.append(dict(e=engine, f=emit, r=tuple(reads), w=tuple(writes),
                             dma=dma, q=(q or engine), tag=tag, deps=set(),
                             inc=False, cnt=None, waits=[]))

    def schedule(self):
        lw, rd = {}, {}
        for i, op in enumerate(self.ops):
            deps = set()
            for r in op["r"]:
                if r in lw:
                    deps.add(lw[r])
            for w in op["w"]:
                if w in lw:
                    deps.add(lw[w])
                for j in rd.get(w, ()):
                    deps.add(j)
            deps.discard(i)
            op["deps"] = deps
            for r in op["r"]:
                rd.setdefault(r, []).append(i)
            for w in op["w"]:
                lw[w] = i
                rd[w] = []
        for op in self.ops:
            for j in op["deps"]:
                self.ops[j]["inc"] = True
        ccnt = {e: 0 for e in self.ENGINES}
        qcnt = {qn: 0 for qn in self.QUEUES}
        for op in self.ops:
            if op["dma"]:
                qcnt[op["q"]] += 16
                op["cnt"] = qcnt[op["q"]]
                op["inc"] = True
            elif op["inc"]:
                ccnt[op["e"]] += 1
                op["cnt"] = ccnt[op["e"]]
        seen = {e: {} for e in self.ENGINES}
        for op in self.ops:
            mx = {}
            for j in sorted(op["deps"]):
                pj = self.ops[j]
                key = ("d", pj["q"]) if pj["dma"] else ("c", pj["e"])
                if pj["cnt"] > mx.get(key, 0):
                    mx[key] = pj["cnt"]
            s = seen[op["e"]]
            for key, val in sorted(mx.items()):
                if s.get(key, 0) < val:
                    s[key] = val
                    op["waits"].append((key, val))


def _build_program():
    nc = bass.Bass("TRN2", target_bir_lowering=False)
    p = Prog()

    x_in = nc.dram_tensor("x_in", [TT, D], F32, kind="ExternalInput")
    wq_t = nc.dram_tensor("wq_t", [D, D], F32, kind="ExternalInput")
    wk_t = nc.dram_tensor("wk_t", [D, KVH * HD], F32, kind="ExternalInput")
    wv_t = nc.dram_tensor("wv_t", [D, KVH * HD], F32, kind="ExternalInput")
    wo_t = nc.dram_tensor("wo_t", [D, D], F32, kind="ExternalInput")
    wg_in = nc.dram_tensor("wg_in", [II, D], F32, kind="ExternalInput")
    wu_in = nc.dram_tensor("wu_in", [II, D], F32, kind="ExternalInput")
    wd_in = nc.dram_tensor("wd_in", [D, II], F32, kind="ExternalInput")
    qg_in = nc.dram_tensor("qg_in", [1, H], F32, kind="ExternalInput")
    pm_in = nc.dram_tensor("pm_in", [1, 32], F32, kind="ExternalInput")
    cos_in = nc.dram_tensor("cos_in", [TT, 32], F32, kind="ExternalInput")
    sin_in = nc.dram_tensor("sin_in", [TT, 32], F32, kind="ExternalInput")
    msk_in = nc.dram_tensor("msk_in", [3, 128, 384], BF16, kind="ExternalInput")
    idf_in = nc.dram_tensor("idf_in", [128, 128], F32, kind="ExternalInput")
    idb_in = nc.dram_tensor("idb_in", [128, 128], BF16, kind="ExternalInput")
    y_out = nc.dram_tensor("y_out", [TO, D], F32, kind="ExternalOutput")

    dbg = "ExternalOutput" if _DEBUG else "Internal"
    qT_d = nc.dram_tensor("qT_d", [8, 128, TO], BF16)
    kT_d = nc.dram_tensor("kT_d", [4, 128, TT], BF16)
    v_d = nc.dram_tensor("v_d", [NT, 128, 260], BF16)
    xqT_d = nc.dram_tensor("xqT_d", [8, 128, TO], BF16)
    acc_d = nc.dram_tensor("acc_d", [TO, D], F32, kind=dbg)
    acc2_d = nc.dram_tensor("acc2_d", [TO, D], F32, kind=dbg)
    xq_dbg = (nc.dram_tensor("xq_dbg", [TO, D], BF16, kind="ExternalOutput")
              if _DEBUG else None)
    ym_dbg = (nc.dram_tensor("ym_dbg", [TO, D], BF16, kind="ExternalOutput")
              if _DEBUG else None)
    e2_dbg = (nc.dram_tensor("e2_dbg", [4, 128, 1536], BF16, kind="ExternalOutput")
              if _DEBUG else None)
    qk_dbg = (nc.dram_tensor("qk_dbg", [2, 128, 1536], BF16, kind="ExternalOutput")
              if _DEBUG else None)
    wgq_d = nc.dram_tensor("wgq_d", [D, II], BF16)
    wuq_d = nc.dram_tensor("wuq_d", [D, II], BF16)
    wdq_d = nc.dram_tensor("wdq_d", [II, D], BF16)

    pb = [nc.alloc_psum_tensor(f"pb{i}", [128, 512], F32) for i in range(8)]
    pbb = [t.bitcast(BF16) for t in pb]
    PS = [f"ps{i}" for i in range(8)]

    def sb(name, shape, dtype):
        return nc.alloc_sbuf_tensor(name, shape, dtype)

    ident_f = sb("ident_f", [128, 128], F32)
    ident_b = sb("ident_b", [128, 128], BF16)
    msk_sb = sb("msk_sb", [128, 3 * 384], BF16)
    gb_sb = sb("gb_sb", [128, 32], F32)
    pm_sb = sb("pm_sb", [128, 32], F32)
    row_sb = sb("row_sb", [1, 64], F32)
    ones1 = sb("ones1", [1, 128], F32)
    c175 = sb("c175", [128, 1], F32)
    eps5 = sb("eps5", [128, 1], F32)
    eps6 = sb("eps6", [128, 1], F32)
    lnh = sb("lnh", [128, 1], F32)
    QFD = IHALF
    t_s = sb("t_s", [128, QFD], F32)
    qlo_s = sb("qlo_s", [128, QFD], F32)
    q2_s = t_s
    dq_s = qlo_s

    bm_s = sb("bm_s", [128, QFD // 16], F32)
    sc_s = sb("sc_s", [128, QFD // 16], F32)
    sh_s = sb("sh_s", [128, QFD // 16], F32)
    r_s = sb("r_s", [128, QFD // 16], F32)
    stage = [sb(f"stage{i}", [128, 1024], F32) for i in range(2)]
    xres_sb = [sb(f"xres{i}", [128, 1024], F32) for i in range(2)]
    ostage = [sb(f"ostage{i}", [128, 1024], F32) for i in range(2)]
    bn_s = sb("bn_s", [128, 12], F32)
    mv_s = sb("mv_s", [128, 2], F32)
    r1_s = sb("r1_s", [128, 1], F32)
    r2_s = sb("r2_s", [128, 1], F32)
    nm_s = sb("nm_s", [128, 1], F32)
    rk_s = sb("rk_s", [128, 20], F32)
    rk2_s = sb("rk2_s", [128, 20], F32)

    kt_sb = [sb(f"kt_sb{i}", [128, 1536], BF16) for i in range(2)]
    v2_sb = [sb(f"v2_sb{i}", [128, 780], BF16) for i in range(2)]


    rp_sb = sb("rp_sb", [128, 16], F32)
    sall_sb = sb("sall_sb", [128, 32], F32)

    UNION_BYTES = 139 * 1024
    union = nc.alloc_sbuf_tensor("union", [128, UNION_BYTES], U8)
    ubase = nc.lookup_mloc(union).addr

    class Ar:
        def __init__(self):
            self.off = 0
            self.keys = []

        def t(self, name, shape, dtype, key=None):
            per = int(np.prod(shape[1:])) * mybir.dt.size(dtype)
            h = nc.virtual_sbuf_tensor(name, shape, dtype)
            nc._tensor(h.name, h.shape, h.dtype, type="SB")
            mloc = nc.lookup_mloc(h)
            mloc.allocated = True
            mloc.addr = ubase + self.off
            self.off = (self.off + per + 31) & ~31
            assert self.off <= UNION_BYTES, f"arena overflow {self.off}"
            self.keys.append(key or name)
            return h

    arA = Ar()
    cosr = [arA.t(f"cosr{i}", [128, 32], F32, f"cos{i}") for i in range(4)]
    sinr = [arA.t(f"sinr{i}", [128, 32], F32, f"sin{i}") for i in range(4)]
    xn_t = [arA.t(f"xn{i}", [128, 1024], F32) for i in range(2)]
    xnT_t = [arA.t(f"xnT{i}", [128, 1024], BF16) for i in range(3)]
    xq_t = [arA.t(f"xq{i}", [128, 1024], BF16) for i in range(2)]
    xqT_t = [arA.t(f"xqT{i}", [128, 1024], BF16) for i in range(2)]
    sq_t = [arA.t(f"sqt{i}", [128, 1280], BF16, f"sq{i}") for i in range(2)]
    qn_t = [arA.t(f"qn{i}", [128, 1024], F32) for i in range(2)]
    qr_t = [arA.t(f"qr{i}", [128, 1024], BF16) for i in range(2)]
    qT_t = [arA.t(f"qTt{i}", [128, 1024], BF16, f"qT{i}") for i in range(2)]
    rt1 = arA.t("rt1", [128, 512], F32)
    rt2 = arA.t("rt2", [128, 512], F32)
    kn_t = [arA.t(f"kn{i}", [128, 256], F32) for i in range(2)]
    kr_t = [arA.t(f"kr{i}", [128, 256], BF16) for i in range(2)]
    krd_t = [arA.t(f"krd{i}", [128, 512], BF16, f"krd{i}") for i in range(2)]
    kT_t = [arA.t(f"kTt{i}", [128, 512], BF16, f"kT{i}") for i in range(2)]
    v_t = [arA.t(f"vt{i}", [128, 260], BF16, f"v{i}") for i in range(2)]
    wqT_sb = arA.t("wqT_sb", [128, 8 * 1024], BF16)
    wkT_sb = arA.t("wkT_sb", [128, 8 * 256], BF16)
    wvT_sb = arA.t("wvT_sb", [128, 8 * 256], BF16)
    woT_sb = sb("woT_sb", [128, 8 * 1024], BF16)
    qt_sb = [arA.t(f"qt_sb{i}", [128, 1024], BF16, f"qts{i}") for i in range(2)]
    y65_sb = [arA.t(f"y65_sb{i}", [128, 1040], F32, f"y65{i}") for i in range(2)]
    ymix_sb = [arA.t(f"ymix_sb{i}", [128, 1024], BF16, f"ymix{i}") for i in range(2)]
    ymixT_sb = [arA.t(f"ymixT_sb{i}", [128, 1024], BF16, f"ymixT{i}")
                for i in range(2)]
    e_sb = [[arA.t(f"e_sb{i}_{j}", [128, 1536], BF16, f"e{i}{j}")
             for j in range(2)] for i in range(2)]
    t_w = arA.t("t_w", [128, QFD], F32)
    qlo_w = arA.t("qlo_w", [128, QFD], F32)
    bm_w = arA.t("bm_w", [128, QFD // 16], F32)
    sc_w = arA.t("sc_w", [128, QFD // 16], F32)
    sh_w = arA.t("sh_w", [128, QFD // 16], F32)
    r_w = arA.t("r_w", [128, QFD // 16], F32)
    wsq_sb = arA.t("wsq_sb", [128, 2816], BF16)
    wstg = arA.t("wstg", [128, 1408], F32)
    wtT_sb = arA.t("wtT_sb", [128, 1024], BF16)

    arB = Ar()
    wg_sb = arB.t("wg_sb", [128, 8 * IHALF], BF16)
    wu_sb = arB.t("wu_sb", [128, 8 * IHALF], BF16)
    wd_sb = arB.t("wd_sb", [128, NCH * 1024], BF16)
    xqs_sb = [arB.t(f"xqs{i}", [128, 1024], BF16) for i in range(2)]
    ain_sb = [arB.t(f"ain{i}", [128, 1024], F32) for i in range(2)]
    sg_sb = [arB.t(f"sg_sb{i}", [128, IHALF], F32) for i in range(2)]
    h_sb = [arB.t(f"h_sb{i}", [128, IHALF], F32) for i in range(4)]
    hq_sb = [arB.t(f"hq_sb{i}", [128, IHALF], BF16) for i in range(2)]
    hqT_sb = [arB.t(f"hqT_sb{i}", [128, IHALF], BF16) for i in range(2)]

    def dma(qe, out_ap, in_ap, reads, writes, tag=""):
        p.add(qe, lambda e, o=out_ap, i=in_ap: e.dma_start(o, i),
              reads=reads, writes=writes, dma=True, tag=tag)

    def quant_chain(src_ap, out_ap, fd, src_key, out_key, pr=128, deq_eng="dve",
                    scr="a"):
        nb = fd // 16
        if scr == "w":
            tS, qloS, q2S, dqS = t_w, qlo_w, t_w, qlo_w
            bmS, scS, shS, rS = bm_w, sc_w, sh_w, r_w
            kT, kQ = "t_w", "qlo_w"
            kB, kC, kH, kR = "bm_w", "sc_w", "sh_w", "r_w"
        else:
            tS, qloS, q2S, dqS = t_s, qlo_s, q2_s, dq_s
            bmS, scS, shS, rS = bm_s, sc_s, sh_s, r_s
            kT, kQ = "t_s", "qlo_s"
            kB, kC, kH, kR = "bm_s", "sc_s", "sh_s", "r_s"
        src3 = AP(src_ap.tensor, src_ap.offset,
                  [list(src_ap.ap[0]), [16, nb], [1, 16]])
        p.add("dve", lambda e, s=src3, nb=nb, pr=pr: e.tensor_reduce(
            bmS[0:pr, 0:nb], s, AX.X, ALU.max, apply_absolute_value=True),
            reads=(src_key,), writes=(kB,))
        p.add("dve", lambda e, nb=nb, pr=pr: e.tensor_scalar(
            scS[0:pr, 0:nb], bmS[0:pr, 0:nb], 1.0 / 6.0, 1e-12, ALU.mult, ALU.max),
            reads=(kB,), writes=(kC,))
        p.add("dve", lambda e, nb=nb, pr=pr: e.reciprocal(
            rS[0:pr, 0:nb], scS[0:pr, 0:nb]),
            reads=(kC,), writes=(kR,))
        p.add("dve", lambda e, nb=nb, pr=pr: e.tensor_scalar_mul(
            shS[0:pr, 0:nb], scS[0:pr, 0:nb], 0.5),
            reads=(kC,), writes=(kH,))
        rbc = AP(rS, 0, [[QFD // 16, pr], [1, nb], [0, 16]])
        sbc = AP(shS, 0, [[QFD // 16, pr], [1, nb], [0, 16]])
        p.add("dve", lambda e, s=src_ap, r=rbc, fd=fd, pr=pr: e._custom_dve(
            T_ABS, out=tS[0:pr, 0:fd], in0=s, in1=r),
            reads=(src_key, kR), writes=(kT,))
        p.add("dve", lambda e, fd=fd, pr=pr: e._custom_dve(
            SNAP_LO, out=qloS[0:pr, 0:fd], in0=tS[0:pr, 0:fd], in1=c175[0:pr, 0:1],
            s0=0.25, s1=0.75, imm2=1.25),
            reads=(kT, "c175"), writes=(kQ,))
        p.add("dve", lambda e, fd=fd, pr=pr: e._custom_dve(
            SNAP_HI, out=q2S[0:pr, 0:fd], in0=tS[0:pr, 0:fd], in1=qloS[0:pr, 0:fd],
            s0=2.5, s1=3.5, imm2=5.0),
            reads=(kT, kQ), writes=(kT,))
        p.add(deq_eng, lambda e, s=sbc, fd=fd, pr=pr: e.tensor_tensor(
            dqS[0:pr, 0:fd], q2S[0:pr, 0:fd], s, ALU.mult),
            reads=(kT, kH), writes=(kQ,))
        p.add("dve", lambda e, o=out_ap, g=src_ap, fd=fd, pr=pr: e._custom_dve(
            FINAL_SIGN, out=o, in0=dqS[0:pr, 0:fd], in1=g),
            reads=(kQ, src_key), writes=(out_key,))

    def mm_group(emits, reads, writes, tag=""):
        def f(e, emits=tuple(emits)):
            last = None
            for fn in emits:
                last = fn(e)
            return last
        p.add("pe", f, reads=reads, writes=writes, tag=tag)

    def transpose8(src, src_key, dst, dst_key, bank0, n, f32mode):
        step = 4
        for c4 in range(0, n, step):
            nch = min(step, n - c4)
            bank = bank0 + ((c4 // step) % 2)
            for c in range(c4, c4 + nch):
                if f32mode:
                    p.add("pe", lambda e, c=c, c4=c4, bank=bank, src=src: e.transpose(
                        pb[bank][:, (c - c4) * 128:(c - c4 + 1) * 128],
                        src[:, c * 128:(c + 1) * 128], ident_f[:, :]),
                        reads=(src_key, "ident_f"), writes=(PS[bank],))
                else:
                    p.add("pe", lambda e, c=c, c4=c4, bank=bank, src=src: e.transpose(
                        pbb[bank][:, (c - c4) * 128:(c - c4 + 1) * 128],
                        src[:, c * 128:(c + 1) * 128], ident_b[:, :]),
                        reads=(src_key, "ident_b"), writes=(PS[bank],))
            if f32mode:
                p.add("act", lambda e, c4=c4, nch=nch, bank=bank, dst=dst: e.copy(
                    dst[:, c4 * 128:(c4 + nch) * 128], pb[bank][:, 0:nch * 128]),
                    reads=(PS[bank],), writes=(dst_key,))
            else:
                p.add("act", lambda e, c4=c4, nch=nch, bank=bank, dst=dst: e.copy(
                    dst[:, c4 * 128:(c4 + nch) * 128], pbb[bank][:, 0:nch * 128]),
                    reads=(PS[bank],), writes=(dst_key,))

    p.add("dve", lambda e: e.memset(c175[:, :], 1.75), writes=("c175",))
    p.add("dve", lambda e: e.memset(eps5[:, :], 1e-5), writes=("eps5",))
    p.add("dve", lambda e: e.memset(eps6[:, :], 1e-6), writes=("eps6",))
    p.add("dve", lambda e: e.memset(lnh[:, :], -0.6931471805599453), writes=("lnh",))
    p.add("dve", lambda e: e.memset(ones1[:, :], 1.0), writes=("ones1",))
    dma("sp", ident_f[:, :], idf_in[:, :], (), ("ident_f",))
    dma("sp", ident_b[:, :], idb_in[:, :], (), ("ident_b",))
    dma("sp", AP(msk_sb, 0, [[3 * 384, 128], [384, 3], [1, 384]]),
        AP(msk_in, 0, [[384, 128], [128 * 384, 3], [1, 384]]), (), ("msk_sb",))
    dma("sp", row_sb[0:1, 0:H], qg_in[0:1, :], (), ("row_q",))
    dma("sp", row_sb[0:1, 32:64], pm_in[0:1, :], (), ("row_pm",))
    p.add("act", lambda e: e.activation(row_sb[0:1, 16:32], row_sb[0:1, 0:H], AF.Abs),
          reads=("row_q",), writes=("row_a",))
    p.add("dve", lambda e: e.tensor_scalar_mul(
        row_sb[0:1, 16:32], row_sb[0:1, 16:32], -8.0),
        reads=("row_a",), writes=("row_a",))
    p.add("dve", lambda e: e.tensor_scalar_mul(
        row_sb[0:1, 0:16], row_sb[0:1, 0:16], 0.125),
        reads=("row_q",), writes=("row_q2",))
    mm_group([lambda e: e.matmul(pb[6][:, 0:64], ones1[0:1, :], row_sb[0:1, 0:64],
                                 start=True, stop=True)],
             reads=("ones1", "row_a", "row_q2", "row_pm"), writes=(PS[6],))
    p.add("act", lambda e: e.copy(gb_sb[:, 0:16], pb[6][:, 16:32]),
          reads=(PS[6],), writes=("gb_sb",))
    p.add("act", lambda e: e.copy(gb_sb[:, 16:32], pb[6][:, 0:16]),
          reads=(PS[6],), writes=("gb_sb",))
    p.add("act", lambda e: e.copy(pm_sb[:, :], pb[6][:, 32:64]),
          reads=(PS[6],), writes=("pm_sb",))

    for (wdram, wsb, wkey, ncols) in ((wq_t, wqT_sb, "wqT_sb", 1024),
                                      (wk_t, wkT_sb, "wkT_sb", 256),
                                      (wv_t, wvT_sb, "wvT_sb", 256),
                                      (wo_t, woT_sb, "woT_sb", 1024)):
        for c in range(8):
            st = stage[c % 2]
            dma("sp", st[:, 0:ncols], wdram[c * 128:(c + 1) * 128, :],
                (f"stg{c % 2}",), (f"stg{c % 2}",), tag="wcast_in")
            p.add("act", lambda e, st=st, wsb=wsb, c=c, ncols=ncols: e.copy(
                wsb[:, c * ncols:(c + 1) * ncols], st[:, 0:ncols]),
                reads=(f"stg{c % 2}",), writes=(wkey,))

    def quant_weight_full(dram_in, my_out, out_key):
        rows, cols = dram_in.shape
        for r in range(rows // 128):
            for h0 in range(0, cols, IHALF):
                fd = min(IHALF, cols - h0)
                if cols == D:
                    early = r < (rows // 128) // 2
                else:
                    early = h0 == 0
                dma("sp", wstg[:, 0:fd],
                    dram_in[r * 128:(r + 1) * 128, h0:h0 + fd],
                    ("wstg",), ("wstg",),
                    tag="wqheadE" if early else "wqheadL")
                quant_chain(wstg[:, 0:fd], wsq_sb[:, h0:h0 + fd],
                            fd, "wstg", "wsq_sb", deq_eng="pool", scr="w")
                for c4 in range(h0 // 128, (h0 + fd) // 128, 2):
                    nch = min(2, (h0 + fd) // 128 - c4)
                    bank = 2 + (c4 // 2) % 2
                    for c in range(c4, c4 + nch):
                        p.add("pe", lambda e, c=c, c4=c4, bank=bank: e.transpose(
                            pbb[bank][:, 768 + (c - c4) * 128:
                                      768 + (c - c4 + 1) * 128],
                            wsq_sb[:, c * 128:(c + 1) * 128], ident_b[:, :]),
                            reads=("wsq_sb", "ident_b"), writes=(PS[bank],))
                    reg = ((c4 // 2) % 4) * 256
                    p.add("act", lambda e, nch=nch, bank=bank, reg=reg: e.copy(
                        wtT_sb[:, reg:reg + nch * 128],
                        pbb[bank][:, 768:768 + nch * 128]),
                        reads=(PS[bank],), writes=(f"wtT:{(c4 // 2) % 4}",))
                    dma("pool",
                        AP(my_out, c4 * 128 * rows + r * 128,
                           [[rows, 128], [128 * rows, nch], [1, 128]]),
                        AP(wtT_sb, reg, [[1024, 128], [128, nch], [1, 128]]),
                        (f"wtT:{(c4 // 2) % 4}",), (f"{out_key}:{r}:{c4}",),
                        tag="wsh_out")

    wq_start = len(p.ops)
    quant_weight_full(wg_in, wgq_d, "wgq_d")
    quant_weight_full(wu_in, wuq_d, "wuq_d")
    quant_weight_full(wd_in, wdq_d, "wdq_d")
    wq_end = len(p.ops)
    def wgqkeys(hf):
        return tuple(f"wgq_d:{r}:{c4}" for r in range(hf * 11, hf * 11 + 11)
                     for c4 in range(0, D // 128, 2))

    def wuqkeys(hf):
        return tuple(f"wuq_d:{r}:{c4}" for r in range(hf * 11, hf * 11 + 11)
                     for c4 in range(0, D // 128, 2))

    def wdqkeys(hf):
        return tuple(f"wdq_d:{r}:{c4}" for r in range(D // 128)
                     for c4 in range(hf * 11, hf * 11 + 11, 2))

    def rope(src, src_key, dst, dst_key, nh, b3):
        def v(t, off):
            return AP(t, off, [[t.shape[1], 128], [64, nh], [1, 32]])

        def ctab(t):
            return AP(t, 0, [[32, 128], [0, nh], [1, 32]])
        s1a, s2a = v(src, 0), v(src, 32)
        d1a, d2a = v(dst, 0), v(dst, 32)
        cr, sr = ctab(cosr[b3]), ctab(sinr[b3])
        rt1v = AP(rt1, 0, [[512, 128], [32, nh], [1, 32]])
        rt2v = AP(rt2, 0, [[512, 128], [32, nh], [1, 32]])
        p.add("dve", lambda e: e.tensor_tensor(rt1v, s1a, cr, ALU.mult),
              reads=(src_key, f"cos{b3}"), writes=("rt1",))
        p.add("dve", lambda e: e.tensor_tensor(rt2v, s2a, sr, ALU.mult),
              reads=(src_key, f"sin{b3}"), writes=("rt2",))
        p.add("dve", lambda e: e.tensor_tensor(d1a, rt1v, rt2v, ALU.add),
              reads=("rt1", "rt2"), writes=(dst_key,))
        p.add("dve", lambda e: e.tensor_tensor(rt1v, s2a, cr, ALU.mult),
              reads=(src_key, f"cos{b3}"), writes=("rt1",))
        p.add("dve", lambda e: e.tensor_tensor(rt2v, s1a, sr, ALU.mult),
              reads=(src_key, f"sin{b3}"), writes=("rt2",))
        p.add("dve", lambda e: e.tensor_tensor(d2a, rt1v, rt2v, ALU.subtract),
              reads=("rt1", "rt2"), writes=(dst_key,))

    def p1_stage_a(it):
        bf = it % 2
        b3 = it % 3
        own = it >= 2
        xs = stage[bf]
        dma("sp", xs[:, :], x_in[it * 128:(it + 1) * 128, :],
            (f"stg{bf}",), (f"stg{bf}",), tag=f"x{it}")
        b4 = it % 4
        dma("sp", cosr[b4][:, :], cos_in[it * 128:(it + 1) * 128, :],
            (f"cos{b4}",), (f"cos{b4}",))
        dma("sp", sinr[b4][:, :], sin_in[it * 128:(it + 1) * 128, :],
            (f"sin{b4}",), (f"sin{b4}",))
        p.add("dve", lambda e: e.bn_stats(bn_s[:, 0:6], xs[:, 0:512]),
              reads=(f"stg{bf}",), writes=("bn_s",))
        p.add("dve", lambda e: e.bn_stats(bn_s[:, 6:12], xs[:, 512:1024]),
              reads=(f"stg{bf}",), writes=("bn_s",))
        p.add("dve", lambda e: e.bn_aggr(mv_s[:, :], bn_s[:, :]),
              reads=("bn_s",), writes=("mv_s",))
        p.add("act", lambda e: e.activation(r1_s[:, :], mv_s[:, 1:2], AF.Ln,
                                            bias=eps5[:, 0:1], scale=1.0),
              reads=("mv_s", "eps5"), writes=("r1_s",))
        p.add("act", lambda e: e.activation(r2_s[:, :], r1_s[:, :], AF.Exp,
                                            bias=0.0, scale=-0.5),
              reads=("r1_s",), writes=("r2_s",))
        p.add("dve", lambda e: e.tensor_scalar(nm_s[:, :], mv_s[:, 0:1],
                                               r2_s[:, 0:1], -1.0,
                                               ALU.mult, ALU.mult),
              reads=("mv_s", "r2_s"), writes=("nm_s",))
        p.add("act", lambda e: e.activation(xn_t[bf][:, :], xs[:, :], AF.Identity,
                                            bias=nm_s[:, 0:1], scale=r2_s[:, 0:1]),
              reads=(f"stg{bf}", "nm_s", "r2_s"), writes=(f"xn{bf}",))
        transpose8(xn_t[bf], f"xn{bf}", xnT_t[b3], f"xnT{b3}", 0, 8, True)
        if own:
            quant_chain(xn_t[bf][:, :], xq_t[bf][:, :], 1024, f"xn{bf}",
                        f"xq{bf}", deq_eng="dve")
            if _DEBUG:
                dma("pool", xq_dbg[(it - 2) * 128:(it - 1) * 128, :],
                    xq_t[bf][:, :], (f"xq{bf}",), (f"xqdbg:{it}",))

    def p1_stage_a2(it):
        bf = it % 2
        transpose8(xq_t[bf], f"xq{bf}", xqT_t[bf], f"xqT{bf}", 2, 8, False)
        dma("pool",
            AP(xqT_d, (it - 2) * 128, [[TO, 128], [128 * TO, 8], [1, 128]]),
            AP(xqT_t[bf], 0, [[1024, 128], [128, 8], [1, 128]]),
            (f"xqT{bf}",), (f"xqT_d:{it - 2}",), tag="xqT_out")

    def p1_stage_b1(it):
        bf = it % 2
        b3 = it % 3
        own = it >= 2

        def kv_mms(wsb, wkey, bank):
            ems = []
            for c in range(8):
                ems.append(lambda e, c=c, wsb=wsb, bank=bank: e.matmul(
                    pb[bank][:, 0:256], xnT_t[b3][:, c * 128:(c + 1) * 128],
                    wsb[:, c * 256:(c + 1) * 256],
                    start=(c == 0), stop=(c == 7)))
            mm_group(ems, reads=(f"xnT{b3}", wkey), writes=(PS[bank],), tag="kvmm")

        kv_mms(wkT_sb, "wkT_sb", 6)
        if own:
            for half in range(2):
                ems = []
                for c in range(8):
                    ems.append(lambda e, c=c, half=half: e.matmul(
                        pb[4 + half][:, 0:512],
                        xnT_t[b3][:, c * 128:(c + 1) * 128],
                        wqT_sb[:, c * 1024 + half * 512: c * 1024 + half * 512 + 512],
                        start=(c == 0), stop=(c == 7)))
                mm_group(ems, reads=(f"xnT{b3}", "wqT_sb"),
                         writes=(PS[4 + half],), tag="qmm")
        p.add("act", lambda e, bf=bf: e.activation(
            sq_t[bf][:, 1024:1280], AP(pb[6], 0, [[512, 128], [1, 256]]),
            AF.Square), reads=(PS[6],), writes=(f"sq{bf}",))
        if own:
            for half in range(2):
                p.add("act", lambda e, half=half, bf=bf: e.activation(
                    sq_t[bf][:, half * 512:(half + 1) * 512],
                    AP(pb[4 + half], 0, [[512, 128], [1, 512]]), AF.Square),
                    reads=(PS[4 + half],), writes=(f"sq{bf}",))
        p.add("act", lambda e, bf=bf: e.copy(
            kn_t[bf][:, :], AP(pb[6], 0, [[512, 128], [1, 256]])),
            reads=(PS[6],), writes=(f"kn{bf}",))
        if own:
            for half in range(2):
                p.add("act", lambda e, half=half, bf=bf: e.copy(
                    AP(qn_t[bf], half * 512, [[1024, 128], [1, 512]]),
                    AP(pb[4 + half], 0, [[512, 128], [1, 512]])),
                    reads=(PS[4 + half],), writes=(f"qn{bf}",))
        kv_mms(wvT_sb, "wvT_sb", 7)
        p.add("act", lambda e: e.copy(
            AP(v_t[bf], 0, [[260, 128], [65, 4], [1, 64]]),
            AP(pb[7], 0, [[512, 128], [64, 4], [1, 64]])),
            reads=(PS[7], f"v{bf}"), writes=(f"v{bf}",))
        dma("pool", v_d[it, :, :], v_t[bf][:, :], (f"v{bf}",), (f"v_d:{it}",))

    def p1_stage_b2_head(it):
        bf = it % 2
        own = it >= 2
        nh_tot = 20 if own else 4
        sq3 = AP(sq_t[bf], 1024 if not own else 0,
                 [[1280, 128], [64, nh_tot], [1, 64]])
        p.add("dve", lambda e, sq3=sq3, nh=nh_tot: e.tensor_reduce(
            rk_s[:, 0:nh], sq3, AX.X, ALU.add),
            reads=(f"sq{bf}",), writes=("rk_s",))
        p.add("act", lambda e, nh=nh_tot: e.activation(
            rk2_s[:, 0:nh], rk_s[:, 0:nh], AF.Ln, bias=eps6[:, 0:1], scale=1.0 / 64),
            reads=("rk_s", "eps6"), writes=("rk2_s",))
        p.add("act", lambda e, nh=nh_tot: e.activation(
            rk_s[:, 0:nh], rk2_s[:, 0:nh], AF.Exp, bias=0.0, scale=-0.5),
            reads=("rk2_s",), writes=("rk_s",))

    def p1_stage_b2(it):
        bf = it % 2
        b4 = it % 4
        own = it >= 2
        koff = 16 if own else 0
        rkbc = AP(rk_s, koff, [[20, 128], [1, 4], [0, 64]])
        p.add("dve", lambda e, bf=bf, rkbc=rkbc: e.tensor_tensor(
            kn_t[bf][:, :], kn_t[bf][:, :], rkbc, ALU.mult),
            reads=(f"kn{bf}", "rk_s"), writes=(f"kn{bf}",))
        rope(kn_t[bf], f"kn{bf}", kr_t[bf], f"kr{bf}", 4, b4)
        p.add("dve", lambda e: e.tensor_copy(
            AP(krd_t[bf], 0, [[512, 128], [128, 4], [64, 2], [1, 64]]),
            AP(kr_t[bf], 0, [[256, 128], [64, 4], [0, 2], [1, 64]])),
            reads=(f"kr{bf}",), writes=(f"krd{bf}",))
        if own:
            for half in range(2):
                rbc = AP(rk_s, half * 8, [[20, 128], [1, 8], [0, 64]])
                p.add("dve", lambda e, half=half, bf=bf, rbc=rbc: e.tensor_tensor(
                    AP(qn_t[bf], half * 512, [[1024, 128], [1, 512]]),
                    AP(qn_t[bf], half * 512, [[1024, 128], [1, 512]]),
                    rbc, ALU.mult),
                    reads=(f"qn{bf}", "rk_s"), writes=(f"qn{bf}",))
            rope(qn_t[bf], f"qn{bf}", qr_t[bf], f"qr{bf}", 16, b4)

    def p1_stage_b3(it):
        bf = it % 2
        own = it >= 2
        for c in range(4):
            p.add("pe", lambda e, c=c: e.transpose(
                pbb[3][:, 0:128], krd_t[bf][:, c * 128:(c + 1) * 128],
                ident_b[:, :]),
                reads=(f"krd{bf}", "ident_b"), writes=(PS[3],))
            p.add("act", lambda e, c=c: e.copy(
                kT_t[bf][:, c * 128:(c + 1) * 128], pbb[3][:, 0:128]),
                reads=(PS[3],), writes=(f"kT{bf}",))
        dma("pool",
            AP(kT_d, it * 128, [[TT, 128], [128 * TT, 4], [1, 128]]),
            AP(kT_t[bf], 0, [[512, 128], [128, 4], [1, 128]]),
            (f"kT{bf}",), (f"kT_d:{it}",))
        if own:
            transpose8(qr_t[bf], f"qr{bf}", qT_t[bf], f"qT{bf}", 0, 8, False)
            dma("pool",
                AP(qT_d, (it - 2) * 128, [[TO, 128], [128 * TO, 8], [1, 128]]),
                AP(qT_t[bf], 0, [[1024, 128], [128, 8], [1, 128]]),
                (f"qT{bf}",), (f"qT_d:{it - 2}",), tag="qT_out")

    def p2_stage_a(qi):
        bf = qi % 2
        it = qi + 2
        dma("sp", AP(qt_sb[bf], 0, [[1024, 128], [128, 8], [1, 128]]),
            AP(qT_d, qi * 128, [[TO, 128], [128 * TO, 8], [1, 128]]),
            (f"qT_d:{qi}", f"qts{bf}"), (f"qts{bf}",), tag="qt_in")
        dma("sp", AP(kt_sb[bf], 0, [[1536, 128], [384, 4], [1, 384]]),
            AP(kT_d, qi * 128, [[TT, 128], [128 * TT, 4], [1, 384]]),
            tuple([f"kT_d:{j}" for j in range(qi, qi + 3)]) + (f"kts{bf}",),
            (f"kts{bf}",), tag="kt_in")
        dma("sp", AP(v2_sb[bf], 0, [[780, 128], [260, 3], [1, 260]]),
            AP(v_d, qi * 128 * 260, [[260, 128], [128 * 260, 3], [1, 260]]),
            tuple([f"v_d:{j}" for j in range(qi, qi + 3)]) + (f"v2s{bf}",),
            (f"v2s{bf}",), tag="v_in")
        dma("sp", xres_sb[bf][:, :], x_in[it * 128:(it + 1) * 128, :],
            (f"xres{bf}",), (f"xres{bf}",), tag="xres")
        if _DEBUG and qi == 1:
            dma("pool", qk_dbg[0, :, 0:1024], qt_sb[bf][:, :], (f"qts{bf}",), ("qkdbg0",))
            dma("pool", qk_dbg[1, :, :], kt_sb[bf][:, :], (f"kts{bf}",), ("qkdbg1",))

    def p2_scores(qi, g):
        bf = qi % 2
        sel = min(qi, 2)
        if True:
            for k2 in range(2):
                m = 2 * g + k2
                a, b2 = PAIRS[m]
                sb0 = (m % 2) * 2
                kv = a // 4
                for c in range(3):
                    def em(e, c=c, m=m, sb0=sb0, kv=kv, bf=bf):
                        e.matmul(pb[sb0][:, c * 128:(c + 1) * 128],
                                 kt_sb[bf][0:64, kv * 384 + c * 128:
                                           kv * 384 + (c + 1) * 128],
                                 qt_sb[bf][0:64, m * 128:(m + 1) * 128],
                                 start=True, stop=True, tile_position=(0, 0))
                        return e.matmul(
                            pb[sb0 + 1][:, c * 128:(c + 1) * 128],
                            kt_sb[bf][64:128, kv * 384 + c * 128:
                                      kv * 384 + (c + 1) * 128],
                            qt_sb[bf][64:128, m * 128:(m + 1) * 128],
                            start=True, stop=True, tile_position=(64, 0))
                    p.add("pe", em, reads=(f"kts{bf}", f"qts{bf}"),
                          writes=(PS[sb0], PS[sb0 + 1]), tag="score")
                for j, hh in enumerate((a, b2)):
                    s = 2 * k2 + j
                    p.add("act", lambda e, s=s, hh=hh, bk=sb0 + j, bf=bf, g=g:
                          e.activation(
                              e_sb[bf][g % 2][:, s * 384:(s + 1) * 384],
                              AP(pb[bk], 0, [[512, 128], [1, 384]]),
                              AF.Exp, bias=gb_sb[:, hh:hh + 1],
                              scale=gb_sb[:, 16 + hh:17 + hh]),
                          reads=(PS[sb0 + j], "gb_sb"), writes=(f"e{bf}{g % 2}",))
            p.add("dve", lambda e, sel=sel, bf=bf, g=g: e.tensor_tensor(
                AP(e_sb[bf][g % 2], 0, [[1536, 128], [384, 4], [1, 384]]),
                AP(e_sb[bf][g % 2], 0, [[1536, 128], [384, 4], [1, 384]]),
                AP(msk_sb, sel * 384, [[3 * 384, 128], [0, 4], [1, 384]]),
                ALU.mult),
                reads=(f"e{bf}{g % 2}", "msk_sb"), writes=(f"e{bf}{g % 2}",),
                tag="mask")
            if _DEBUG and qi == 1:
                dma("pool", e2_dbg[g, :, :], e_sb[bf][g % 2][:, :],
                    (f"e{bf}{g % 2}",), (f"e2dbg:{g}",), tag="e2dbg")

    def p2_y(qi, g):
        bf = qi % 2
        ybank = 4 + (g % 2)
        if True:
            for k2 in range(2):
                m = 2 * g + k2
                a, b2 = PAIRS[m]
                for j, hh in enumerate((a, b2)):
                    s = 2 * k2 + j
                    kv = hh // 4
                    ems = []
                    for c in range(3):
                        ems.append(lambda e, c=c, s=s, kv=kv, yb=ybank, bf=bf,
                                   g=g: e.matmul(
                                       pb[yb][:, s * 65:(s + 1) * 65],
                                       e_sb[bf][g % 2][:, s * 384 + c * 128:
                                                       s * 384 + (c + 1) * 128],
                                       v2_sb[bf][:, c * 260 + kv * 65:
                                                 c * 260 + (kv + 1) * 65],
                                       start=(c == 0), stop=(c == 2)))
                    mm_group(ems, reads=(f"e{bf}{g % 2}", f"v2s{bf}"),
                             writes=(PS[ybank],), tag="ymm")
            p.add("act", lambda e, g=g, yb=ybank, bf=bf: e.copy(
                AP(y65_sb[bf], 4 * g * 65, [[1040, 128], [1, 260]]),
                AP(pb[yb], 0, [[512, 128], [1, 260]])),
                reads=(PS[ybank],), writes=(f"y65{bf}",))

    def p2_stage_c(qi):
        bf = qi % 2
        p.add("dve", lambda e, bf=bf: e.reciprocal(
            rp_sb[:, :], AP(y65_sb[bf], 64, [[1040, 128], [65, 16]])),
            reads=(f"y65{bf}",), writes=("rp_sb",))
        p.add("dve", lambda e: e.tensor_tensor(
            sall_sb[:, :], AP(rp_sb, 0, [[16, 128], [2, 8], [0, 2], [1, 2]]),
            pm_sb[:, :], ALU.mult),
            reads=("rp_sb", "pm_sb"), writes=("sall_sb",))
        for pp in range(8):
            for o in range(2):
                p.add("dve", lambda e, pp=pp, o=o, bf=bf: e._custom_dve(
                    PMIX,
                    out=ymix_sb[bf][:, (2 * pp + o) * 64:(2 * pp + o + 1) * 64],
                    in0=AP(y65_sb[bf], (2 * pp) * 65, [[1040, 128], [1, 64]]),
                    in1=AP(y65_sb[bf], (2 * pp + 1) * 65, [[1040, 128], [1, 64]]),
                    s0=sall_sb[:, pp * 4 + o * 2: pp * 4 + o * 2 + 1],
                    s1=sall_sb[:, pp * 4 + o * 2 + 1: pp * 4 + o * 2 + 2]),
                    reads=(f"y65{bf}", "sall_sb"), writes=(f"ymix{bf}",),
                    tag="pmix")
        if _DEBUG:
            dma("pool", ym_dbg[qi * 128:(qi + 1) * 128, :], ymix_sb[bf][:, :],
                (f"ymix{bf}",), (f"ymdbg:{qi}",))

    def p2_stage_c1b(qi):
        bf = qi % 2
        transpose8(ymix_sb[bf], f"ymix{bf}", ymixT_sb[bf], f"ymixT{bf}", 0, 8, False)
        for half in range(2):
            ems = []
            for c in range(8):
                ems.append(lambda e, c=c, half=half, bf=bf: e.matmul(
                    pb[6 + half][:, 0:512],
                    ymixT_sb[bf][:, c * 128:(c + 1) * 128],
                    woT_sb[:, c * 1024 + half * 512: c * 1024 + half * 512 + 512],
                    start=(c == 0), stop=(c == 7)))
            mm_group(ems, reads=(f"ymixT{bf}", "woT_sb"),
                     writes=(PS[6 + half],), tag="omm")

    def p2_stage_c2(qi):
        bf = qi % 2
        for half in range(2):
            p.add("dve", lambda e, half=half, bf=bf: e.tensor_tensor(
                ostage[bf][:, half * 512:(half + 1) * 512],
                AP(pb[6 + half], 0, [[512, 128], [1, 512]]),
                xres_sb[bf][:, half * 512:(half + 1) * 512], ALU.add),
                reads=(PS[6 + half], f"xres{bf}"), writes=(f"ost{bf}",))
        dma("pool", acc_d[qi * 128:(qi + 1) * 128, :], ostage[bf][:, :],
            (f"ost{bf}",), (f"acc_d:{qi}",), tag="acc_out")

    def emit_p2_tile(qi, pending_c):
        p2_scores(qi, 0)
        for g in range(4):
            if g + 1 < 4:
                p2_scores(qi, g + 1)
            if g == 2 and pending_c:
                p2_stage_c1b(pending_c[-1])
            p2_y(qi, g)
            if g == 0 and pending_c:
                p2_stage_c(pending_c[-1])
        if pending_c:
            p2_stage_c2(pending_c.pop())
        if qi + 1 < NQ:
            p2_stage_a(qi + 1)
        pending_c.append(qi)
        p.add("sp", lambda e: e.nop(), tag="anchor3")

    pending_c = []
    for i in range(2):
        p.add("dve", lambda e, i=i: e.memset(v_t[i][:, :], 1.0),
              reads=(), writes=(f"v{i}",))
    p1_stage_a(0)
    p1_stage_a(1)
    for it in range(NT):
        if it >= 2:
            p1_stage_b3(it - 2)
        if 2 <= it + 1 < NT:
            p1_stage_a2(it + 1)
        if it + 2 < NT:
            p1_stage_a(it + 2)
        if it >= 1:
            p1_stage_b2_head(it - 1)
        p1_stage_b1(it)
        if it >= 1:
            p1_stage_b2(it - 1)
        p.add("sp", lambda e: e.nop(), tag="anchor1")
        qi = it - 5
        if qi == 0:
            p2_stage_a(0)
        if 0 <= qi < NQ:
            emit_p2_tile(qi, pending_c)
    p1_stage_b2_head(NT - 1)
    p1_stage_b2(NT - 1)
    p1_stage_b3(NT - 2)
    p1_stage_b3(NT - 1)

    for qi in range(NT - 5, NQ):
        emit_p2_tile(qi, pending_c)
    last = pending_c.pop()
    p2_stage_c(last)
    p2_stage_c1b(last)
    p2_stage_c2(last)

    p.add("sp", lambda e: e.nop(), reads=tuple(arA.keys), writes=("arena_free",),
          tag="barrier")

    def build_ffn_weights(hf):
        for c in range(8):
            dma("sp", wg_sb[:, c * IHALF:(c + 1) * IHALF],
                wgq_d[c * 128:(c + 1) * 128, hf * IHALF:(hf + 1) * IHALF],
                wgqkeys(hf) + ("arena_free", f"wg:{c}"), (f"wg:{c}",),
                tag="wgin" if hf == 0 else "wgin2")
            dma("sp", wu_sb[:, c * IHALF:(c + 1) * IHALF],
                wuq_d[c * 128:(c + 1) * 128, hf * IHALF:(hf + 1) * IHALF],
                wuqkeys(hf) + ("arena_free", f"wu:{c}"), (f"wu:{c}",),
                tag="wuin")
        for ch in range(NCH):
            dma("sp", wd_sb[:, ch * 1024:(ch + 1) * 1024],
                wdq_d[hf * IHALF + ch * 128: hf * IHALF + (ch + 1) * 128, :],
                wdqkeys(hf) + ("arena_free", f"wd:{ch}"), (f"wd:{ch}",),
                tag="wdin")

    WGKEYS = tuple(f"wg:{c}" for c in range(8))
    WUKEYS = tuple(f"wu:{c}" for c in range(8))
    WDKEYS = tuple(f"wd:{ch}" for ch in range(NCH))
    REG = ((0, 0, 512), (1, 512, 1024), (2, 1024, 1408))

    def ffn_stage_a(hf, qi):
        bf = qi % 2
        hb = qi % 4
        dma("sp", AP(xqs_sb[bf], 0, [[1024, 128], [128, 8], [1, 128]]),
            AP(xqT_d, qi * 128, [[TO, 128], [128 * TO, 8], [1, 128]]),
            (f"xqT_d:{qi}", "arena_free", f"xqs{bf}"), (f"xqs{bf}",), tag="xq_in")
        for (wsb, wkeys, b0) in ((wg_sb, WGKEYS, 0), (wu_sb, WUKEYS, 3)):
            for (ri, c0, c1) in REG:
                ems = []
                for c in range(8):
                    ems.append(lambda e, c=c, c0=c0, c1=c1, ri=ri, wsb=wsb,
                               b0=b0, bf=bf: e.matmul(
                        pb[b0 + ri][:, 0:c1 - c0],
                        xqs_sb[bf][:, c * 128:(c + 1) * 128],
                        wsb[:, c * IHALF + c0: c * IHALF + c1],
                        start=(c == 0), stop=(c == 7)))
                mm_group(ems, reads=(f"xqs{bf}",) + wkeys,
                         writes=(PS[b0 + ri],), tag="gumm")
        for (ri, c0, c1) in REG:
            p.add("act", lambda e, ri=ri, c0=c0, c1=c1, bf=bf: e.activation(
                sg_sb[bf][:, c0:c1], AP(pb[ri], 0, [[512, 128], [1, c1 - c0]]),
                AF.Silu),
                reads=(PS[ri],), writes=(f"sg_sb{bf}",))

    def ffn_stage_h(hf, qi):
        bf = qi % 2
        hb = qi % 4
        for (ri, c0, c1) in REG:
            p.add("dve", lambda e, ri=ri, c0=c0, c1=c1, bf=bf, hb=hb:
                  e.tensor_tensor(
                      h_sb[hb][:, c0:c1], sg_sb[bf][:, c0:c1],
                      AP(pb[3 + ri], 0, [[512, 128], [1, c1 - c0]]), ALU.mult),
                  reads=(f"sg_sb{bf}", PS[3 + ri]), writes=(f"h_sb{hb}",))

    def ffn_stage_b(hf, qi):
        bf = qi % 2
        hb = qi % 4
        accsrc = acc_d if hf == 0 else acc2_d
        acckey = "acc_d" if hf == 0 else "acc2_d"
        dma("sp", ain_sb[bf][:, :], accsrc[qi * 128:(qi + 1) * 128, :],
            (f"{acckey}:{qi}", "arena_free", f"ain{bf}"), (f"ain{bf}",),
            tag="acc_in")
        quant_chain(h_sb[hb][:, 0:IHALF], hq_sb[bf][:, 0:IHALF], IHALF,
                    f"h_sb{hb}", f"hq_sb{bf}")
        transpose8(hq_sb[bf], f"hq_sb{bf}", hqT_sb[bf], f"hqT_sb{bf}", 6, NCH,
                   False)
        for half in range(2):
            ems = []
            for c in range(NCH):
                ems.append(lambda e, c=c, half=half, bf=bf: e.matmul(
                    pb[6 + half][:, 0:512],
                    hqT_sb[bf][:, c * 128:(c + 1) * 128],
                    wd_sb[:, c * 1024 + half * 512: c * 1024 + half * 512 + 512],
                    start=(c == 0), stop=(c == NCH - 1)))
            mm_group(ems, reads=(f"hqT_sb{bf}",) + WDKEYS,
                     writes=(PS[6 + half],), tag="dmm")
            p.add("dve", lambda e, half=half, bf=bf: e.tensor_tensor(
                ostage[bf][:, half * 512:(half + 1) * 512],
                AP(pb[6 + half], 0, [[512, 128], [1, 512]]),
                ain_sb[bf][:, half * 512:(half + 1) * 512], ALU.add),
                reads=(PS[6 + half], f"ain{bf}"), writes=(f"ost{bf}",))
        dst = acc2_d if hf == 0 else y_out
        dkey = f"acc2_d:{qi}" if hf == 0 else f"out:{qi}"
        dma("pool", dst[qi * 128:(qi + 1) * 128, :], ostage[bf][:, :],
            (f"ost{bf}",), (dkey,), tag="final_out")

    for hf in range(2):
        build_ffn_weights(hf)
        for j in range(2):
            ffn_stage_a(hf, j)
            ffn_stage_h(hf, j)
        for qi in range(NQ):
            if qi + 2 < NQ:
                ffn_stage_a(hf, qi + 2)
            ffn_stage_b(hf, qi)
            if qi + 2 < NQ:
                ffn_stage_h(hf, qi + 2)

    p.add("sp", lambda e: e.nop(), reads=tuple(f"out:{qi}" for qi in range(NQ)))

    wq_ops = p.ops[wq_start:wq_end]
    rest = p.ops[:wq_start] + p.ops[wq_end:]
    heads = [i for i, op in enumerate(wq_ops)
             if op["tag"] in ("wqheadE", "wqheadL")]
    heads.append(len(wq_ops))
    chunks = [wq_ops[heads[i]:heads[i + 1]] for i in range(len(heads) - 1)]
    early_chunks = [ch for ch in chunks if ch[0]["tag"] == "wqheadE"]
    late_chunks = [ch for ch in chunks if ch[0]["tag"] == "wqheadL"]
    FINE = True
    flat = [op for ch in chunks for op in ch]
    if not FINE:
        out, ci = [], 0
        last_anchor = max(i for i, op in enumerate(rest) if op["tag"] in
                          ("anchor1", "anchor3"))
        for i, op in enumerate(rest):
            out.append(op)
            if op["tag"] in ("anchor1", "anchor3") and ci < len(chunks):
                per = 1 if op["tag"] == "anchor1" else 2
                if i == last_anchor:
                    per = len(chunks) - ci
                for ch in chunks[ci:ci + per]:
                    out.extend(ch)
                ci += per
        fin = out.pop(next(i for i, op in enumerate(out) if op is p.ops[-1]))
        out.append(fin)
        p.ops = out
        return _finish(nc, p)
    n_dve = sum(1 for op in rest if op["e"] == "dve"
                and op["tag"] not in ("wqhead",))
    first_b = next(i for i, op in enumerate(rest) if op["tag"] == "wgin")
    n_elig = sum(1 for op in rest[:first_b] if op["e"] == "dve")
    rate = len(flat) / max(1, n_elig)
    out, wi, acc = [], 0, 0.0
    for i, op in enumerate(rest):
        if i == first_b:
            out.extend(flat[wi:])
            wi = len(flat)
        if i < first_b and op["e"] == "dve" and wi < len(flat):
            acc += rate
            while acc >= 1.0 and wi < len(flat):
                out.append(flat[wi])
                wi += 1
                acc -= 1.0
        out.append(op)
    if wi < len(flat):
        out.extend(flat[wi:])
    fin = out.pop(next(i for i, op in enumerate(out) if op is p.ops[-1]))
    out.append(fin)
    p.ops = out
    return _finish(nc, p)


def _finish(nc, p):
    p.schedule()
    es = contextlib.ExitStack()
    sems = {}
    for e in Prog.ENGINES:
        sems[("c", e)] = es.enter_context(nc.semaphore(f"cs_{e}"))
    for qn in Prog.QUEUES:
        sems[("d", qn)] = es.enter_context(nc.semaphore(f"ds_{qn}"))
    block = es.enter_context(nc.Block())

    def emit_engine(eng_obj, ename):
        for op in p.ops:
            if op["e"] != ename:
                continue
            for (key, val) in op["waits"]:
                eng_obj.wait_ge(sems[key], val)
            inst = op["f"](eng_obj)
            if op["inc"]:
                if op["dma"]:
                    inst.then_inc(sems[("d", op["q"])], 16)
                else:
                    inst.then_inc(sems[("c", ename)], 1)

    block.tensor(lambda e: emit_engine(e, "pe"))
    block.vector(lambda e: emit_engine(e, "dve"))
    block.scalar(lambda e: emit_engine(e, "act"))
    block.gpsimd(lambda e: emit_engine(e, "pool"))
    block.sync(lambda e: emit_engine(e, "sp"))
    es.close()
    mybir.codegen_inst_isa_subclasses(nc)
    return nc


_DEBUG = False
_NC_CACHE = None


def _get_nc():
    global _NC_CACHE
    if _NC_CACHE is None:
        _NC_CACHE = _build_program()
    return _NC_CACHE


def _rope_tables():
    inv = 1.0 / (10000.0 ** (np.arange(0, HD, 2, dtype=np.float32) / HD))
    t = np.arange(S, dtype=np.float32)
    fr = np.outer(t, inv)
    return np.cos(fr).astype(np.float32), np.sin(fr).astype(np.float32)


def _masks(seqstart):
    r = np.arange(128)
    m0 = (r[None, :] < r[:, None]).astype(np.float32)
    m2 = (r[None, :] >= r[:, None]).astype(np.float32)
    ones = np.ones((128, 128), np.float32)
    zeros = np.zeros((128, 128), np.float32)
    normal = np.concatenate([m0, ones, m2], axis=1)
    if not seqstart:
        out = np.stack([normal, normal, normal])
    else:
        t0 = np.concatenate([zeros, zeros, m2], axis=1)
        t1 = np.concatenate([zeros, ones, m2], axis=1)
        out = np.stack([t0, t1, normal])
    return out.astype(ml_dtypes.bfloat16)


def kernel(L0_attn_x, L0_attn_Wq, L0_attn_Wk, L0_attn_Wv, L0_attn_Wo,
           L0_attn_q_gain, L0_attn_pair_mix, L0_ffn_w_gate, L0_ffn_w_up,
           L0_ffn_w_down):
    x = np.ascontiguousarray(np.asarray(L0_attn_x, np.float32))
    cos_f, sin_f = _rope_tables()
    nc = _get_nc()
    in_maps = []
    for c in range(NCORE):
        b, q = c // 4, c % 4
        own = x[b, q * TO:(q + 1) * TO]
        halo = (np.zeros((HALO, D), np.float32) if q == 0
                else x[b, q * TO - HALO: q * TO])
        p0 = q * TO - HALO
        pos = np.clip(np.arange(p0, p0 + TT), 0, S - 1)
        m = dict(
            x_in=np.concatenate([halo, own], 0),
            wq_t=np.ascontiguousarray(np.asarray(L0_attn_Wq, np.float32).T),
            wk_t=np.ascontiguousarray(np.asarray(L0_attn_Wk, np.float32).T),
            wv_t=np.ascontiguousarray(np.asarray(L0_attn_Wv, np.float32).T),
            wo_t=np.ascontiguousarray(np.asarray(L0_attn_Wo, np.float32).T),
            wg_in=np.ascontiguousarray(np.asarray(L0_ffn_w_gate, np.float32)),
            wu_in=np.ascontiguousarray(np.asarray(L0_ffn_w_up, np.float32)),
            wd_in=np.ascontiguousarray(np.asarray(L0_ffn_w_down, np.float32)),
            qg_in=np.asarray(L0_attn_q_gain, np.float32).reshape(1, H),
            pm_in=np.asarray(L0_attn_pair_mix, np.float32).reshape(1, 32),
            cos_in=np.ascontiguousarray(cos_f[pos]),
            sin_in=np.ascontiguousarray(sin_f[pos]),
            msk_in=_masks(q == 0),
            idf_in=np.eye(128, dtype=np.float32),
            idb_in=np.eye(128).astype(ml_dtypes.bfloat16),
        )
        in_maps.append(m)
    res = run_bass_kernel_spmd(nc, in_maps, core_ids=list(range(NCORE)))
    out = np.empty((Bb, S, D), np.float32)
    for c in range(NCORE):
        b, q = c // 4, c % 4
        out[b, q * TO:(q + 1) * TO] = res.results[c]["y_out"]
    return out


# version 34
# speedup vs baseline: 1.0393x; 1.0008x over previous
import contextlib

import numpy as np
import ml_dtypes

import concourse.bass as bass
import concourse.mybir as mybir
from concourse.ap import AP
from concourse.bass_utils import run_bass_kernel_spmd
import concourse.dve_ops as dve_ops
from concourse.dve_op import DveOp
from concourse.dve_spec import Spec, Src0, Src1, C0, C1, C2, C3, Zero, One, lower, maxx

F32 = mybir.dt.float32
BF16 = mybir.dt.bfloat16
U8 = mybir.dt.uint8
AF = mybir.ActivationFunctionType
ALU = mybir.AluOpType
AX = mybir.AxisListType

NCORE = 8
Bb, S, D = 2, 8192, 1024
H, KVH, HD = 16, 4, 64
WIN = 256
II = 2816
TO = 2048
HALO = 256
TT = TO + HALO
NT = TT // 128
NQ = TO // 128
IHALF = II // 2
NCH = IHALF // 128
WG_SH = II // NCORE
WD_SH = D // NCORE
PAIRS = [(2 * i, 2 * i + 1) for i in range(8)]

_REG = {}


def _mk_op(name, spec, reference, steal_from):
    if name in _REG:
        return _REG[name]
    shas = {}
    for v in ("v3", "v4"):
        try:
            shas[v] = lower(spec, ver=v, perf_en=False, name=name).sha(v)
        except Exception:
            pass
    op = DveOp(name, spec, subdim=False, uops_sha=shas, reference=reference)
    dve_ops.OPS.append(op)
    dve_ops._SUB_OPCODE_FOR_NAME[name] = dve_ops._SUB_OPCODE_FOR_NAME[steal_from]
    dve_ops.CUSTOM_DVE_SPECS[name] = spec
    dve_ops.CUSTOM_DVE_REFS[name] = reference
    _REG[name] = op
    return op


def _f32(x):
    return np.asarray(x, np.float32)


T_ABS = _mk_op(
    "ANT_T_ABS", Spec(body=maxx(Src0 * Src1, Zero - Src0 * Src1)),
    lambda in0, in1, s0, s1, imm2: _f32(np.abs(_f32(in0) * _f32(in1))),
    "BITWISE_XOR_TT_BV")
SNAP_LO = _mk_op(
    "ANT_SNAP_LO", Spec(body=(Src0 > C0) + (Src0 > C1) + (Src0 > C2) + (Src0 > C3)),
    lambda in0, in1, s0, s1, imm2: _f32(
        _f32(in0 > s0) + _f32(in0 > s1) + _f32(in0 > imm2) + _f32(in0 > in1)),
    "DVEHASH32L_SEED")
_c5 = (Src0 > C0); _c6 = (Src0 > C1); _c7 = (Src0 > C2)
_s1 = _c5 + _c6; _s2 = _s1 + _c7; _s3 = _s2 + _c7; _s4 = _s3 + _s3
SNAP_HI = _mk_op(
    "ANT_SNAP_HI", Spec(body=_s4 + Src1),
    lambda in0, in1, s0, s1, imm2: _f32(
        2.0 * (_f32(in0 > s0) + _f32(in0 > s1) + 2.0 * _f32(in0 > imm2)) + _f32(in1)),
    "DVEHASH32L_BODY")
FINAL_SIGN = _mk_op(
    "ANT_FINAL_SIGN2", Spec(body=Src0 * ((Src1 >= Zero) + (Src1 >= Zero) - One)),
    lambda in0, in1, s0, s1, imm2: _f32(
        _f32(in0) * (2.0 * _f32(in1 >= 0) - 1.0)),
    "DVEHASH32L_DRAIN")
PMIX = _mk_op(
    "ANT_PMIX", Spec(body=Src0 * C0 + Src1 * C1),
    lambda in0, in1, s0, s1, imm2: _f32(_f32(in0) * _f32(s0) + _f32(in1) * _f32(s1)),
    "CODY_WAITE_CASCADE")


class Prog:
    ENGINES = ("pe", "dve", "act", "pool", "sp")
    QUEUES = ("pe", "dve", "act", "pool", "sp", "cc")

    def __init__(self):
        self.ops = []

    def add(self, engine, emit, reads=(), writes=(), dma=False, q=None, tag=""):
        self.ops.append(dict(e=engine, f=emit, r=tuple(reads), w=tuple(writes),
                             dma=dma, q=(q or engine), tag=tag, deps=set(),
                             inc=False, cnt=None, waits=[]))

    def schedule(self):
        lw, rd = {}, {}
        for i, op in enumerate(self.ops):
            deps = set()
            for r in op["r"]:
                if r in lw:
                    deps.add(lw[r])
            for w in op["w"]:
                if w in lw:
                    deps.add(lw[w])
                for j in rd.get(w, ()):
                    deps.add(j)
            deps.discard(i)
            op["deps"] = deps
            for r in op["r"]:
                rd.setdefault(r, []).append(i)
            for w in op["w"]:
                lw[w] = i
                rd[w] = []
        for op in self.ops:
            for j in op["deps"]:
                self.ops[j]["inc"] = True
        ccnt = {e: 0 for e in self.ENGINES}
        qcnt = {qn: 0 for qn in self.QUEUES}
        for op in self.ops:
            if op["dma"]:
                qcnt[op["q"]] += 16
                op["cnt"] = qcnt[op["q"]]
                op["inc"] = True
            elif op["inc"]:
                ccnt[op["e"]] += 1
                op["cnt"] = ccnt[op["e"]]
        seen = {e: {} for e in self.ENGINES}
        for op in self.ops:
            mx = {}
            for j in sorted(op["deps"]):
                pj = self.ops[j]
                key = ("d", pj["q"]) if pj["dma"] else ("c", pj["e"])
                if pj["cnt"] > mx.get(key, 0):
                    mx[key] = pj["cnt"]
            s = seen[op["e"]]
            for key, val in sorted(mx.items()):
                if s.get(key, 0) < val:
                    s[key] = val
                    op["waits"].append((key, val))


def _build_program():
    nc = bass.Bass("TRN2", target_bir_lowering=False)
    p = Prog()

    x_in = nc.dram_tensor("x_in", [TT, D], F32, kind="ExternalInput")
    wq_t = nc.dram_tensor("wq_t", [D, D], F32, kind="ExternalInput")
    wk_t = nc.dram_tensor("wk_t", [D, KVH * HD], F32, kind="ExternalInput")
    wv_t = nc.dram_tensor("wv_t", [D, KVH * HD], F32, kind="ExternalInput")
    wo_t = nc.dram_tensor("wo_t", [D, D], F32, kind="ExternalInput")
    wg_in = nc.dram_tensor("wg_in", [II, D], F32, kind="ExternalInput")
    wu_in = nc.dram_tensor("wu_in", [II, D], F32, kind="ExternalInput")
    wd_in = nc.dram_tensor("wd_in", [D, II], F32, kind="ExternalInput")
    qg_in = nc.dram_tensor("qg_in", [1, H], F32, kind="ExternalInput")
    pm_in = nc.dram_tensor("pm_in", [1, 32], F32, kind="ExternalInput")
    cos_in = nc.dram_tensor("cos_in", [TT, 32], F32, kind="ExternalInput")
    sin_in = nc.dram_tensor("sin_in", [TT, 32], F32, kind="ExternalInput")
    msk_in = nc.dram_tensor("msk_in", [3, 128, 384], BF16, kind="ExternalInput")
    idf_in = nc.dram_tensor("idf_in", [128, 128], F32, kind="ExternalInput")
    idb_in = nc.dram_tensor("idb_in", [128, 128], BF16, kind="ExternalInput")
    y_out = nc.dram_tensor("y_out", [TO, D], F32, kind="ExternalOutput")

    dbg = "ExternalOutput" if _DEBUG else "Internal"
    qT_d = nc.dram_tensor("qT_d", [8, 128, TO], BF16)
    kT_d = nc.dram_tensor("kT_d", [4, 128, TT], BF16)
    v_d = nc.dram_tensor("v_d", [NT, 128, 260], BF16)
    xqT_d = nc.dram_tensor("xqT_d", [8, 128, TO], BF16)
    acc_d = nc.dram_tensor("acc_d", [TO, D], F32, kind=dbg)
    acc2_d = nc.dram_tensor("acc2_d", [TO, D], F32, kind=dbg)
    xq_dbg = (nc.dram_tensor("xq_dbg", [TO, D], BF16, kind="ExternalOutput")
              if _DEBUG else None)
    ym_dbg = (nc.dram_tensor("ym_dbg", [TO, D], BF16, kind="ExternalOutput")
              if _DEBUG else None)
    e2_dbg = (nc.dram_tensor("e2_dbg", [4, 128, 1536], BF16, kind="ExternalOutput")
              if _DEBUG else None)
    qk_dbg = (nc.dram_tensor("qk_dbg", [2, 128, 1536], BF16, kind="ExternalOutput")
              if _DEBUG else None)
    wgq_d = nc.dram_tensor("wgq_d", [D, II], BF16)
    wuq_d = nc.dram_tensor("wuq_d", [D, II], BF16)
    wdq_d = nc.dram_tensor("wdq_d", [II, D], BF16)

    pb = [nc.alloc_psum_tensor(f"pb{i}", [128, 512], F32) for i in range(8)]
    pbb = [t.bitcast(BF16) for t in pb]
    PS = [f"ps{i}" for i in range(8)]

    def sb(name, shape, dtype):
        return nc.alloc_sbuf_tensor(name, shape, dtype)

    ident_f = sb("ident_f", [128, 128], F32)
    ident_b = sb("ident_b", [128, 128], BF16)
    msk_sb = sb("msk_sb", [128, 3 * 384], BF16)
    gb_sb = sb("gb_sb", [128, 32], F32)
    pm_sb = sb("pm_sb", [128, 32], F32)
    row_sb = sb("row_sb", [1, 64], F32)
    ones1 = sb("ones1", [1, 128], F32)
    c175 = sb("c175", [128, 1], F32)
    eps5 = sb("eps5", [128, 1], F32)
    eps6 = sb("eps6", [128, 1], F32)
    lnh = sb("lnh", [128, 1], F32)
    QFD = IHALF
    t_s = sb("t_s", [128, QFD], F32)
    qlo_s = sb("qlo_s", [128, QFD], F32)
    q2_s = t_s
    dq_s = qlo_s

    bm_s = sb("bm_s", [128, QFD // 16], F32)
    sc_s = sb("sc_s", [128, QFD // 16], F32)
    sh_s = sb("sh_s", [128, QFD // 16], F32)
    r_s = sb("r_s", [128, QFD // 16], F32)
    stage = [sb(f"stage{i}", [128, 1024], F32) for i in range(2)]
    xres_sb = [sb(f"xres{i}", [128, 1024], F32) for i in range(2)]
    ostage = [sb(f"ostage{i}", [128, 1024], F32) for i in range(2)]
    bn_s = sb("bn_s", [128, 12], F32)
    mv_s = sb("mv_s", [128, 2], F32)
    r1_s = sb("r1_s", [128, 1], F32)
    r2_s = sb("r2_s", [128, 1], F32)
    nm_s = sb("nm_s", [128, 1], F32)
    rk_s = sb("rk_s", [128, 20], F32)
    rk2_s = sb("rk2_s", [128, 20], F32)

    kt_sb = [sb(f"kt_sb{i}", [128, 1536], BF16) for i in range(2)]
    v2_sb = [sb(f"v2_sb{i}", [128, 780], BF16) for i in range(2)]


    rp_sb = sb("rp_sb", [128, 16], F32)
    sall_sb = sb("sall_sb", [128, 32], F32)

    UNION_BYTES = 139 * 1024
    union = nc.alloc_sbuf_tensor("union", [128, UNION_BYTES], U8)
    ubase = nc.lookup_mloc(union).addr

    class Ar:
        def __init__(self):
            self.off = 0
            self.keys = []

        def t(self, name, shape, dtype, key=None):
            per = int(np.prod(shape[1:])) * mybir.dt.size(dtype)
            h = nc.virtual_sbuf_tensor(name, shape, dtype)
            nc._tensor(h.name, h.shape, h.dtype, type="SB")
            mloc = nc.lookup_mloc(h)
            mloc.allocated = True
            mloc.addr = ubase + self.off
            self.off = (self.off + per + 31) & ~31
            assert self.off <= UNION_BYTES, f"arena overflow {self.off}"
            self.keys.append(key or name)
            return h

    arA = Ar()
    cosr = [arA.t(f"cosr{i}", [128, 32], F32, f"cos{i}") for i in range(4)]
    sinr = [arA.t(f"sinr{i}", [128, 32], F32, f"sin{i}") for i in range(4)]
    xn_t = [arA.t(f"xn{i}", [128, 1024], F32) for i in range(2)]
    xnT_t = [arA.t(f"xnT{i}", [128, 1024], BF16) for i in range(3)]
    xq_t = [arA.t(f"xq{i}", [128, 1024], BF16) for i in range(2)]
    xqT_t = [arA.t(f"xqT{i}", [128, 1024], BF16) for i in range(2)]
    sq_t = [arA.t(f"sqt{i}", [128, 1280], BF16, f"sq{i}") for i in range(2)]
    qn_t = [arA.t(f"qn{i}", [128, 1024], F32) for i in range(2)]
    qr_t = [arA.t(f"qr{i}", [128, 1024], BF16) for i in range(2)]
    qT_t = [arA.t(f"qTt{i}", [128, 1024], BF16, f"qT{i}") for i in range(2)]
    rt1 = arA.t("rt1", [128, 512], F32)
    rt2 = arA.t("rt2", [128, 512], F32)
    kn_t = [arA.t(f"kn{i}", [128, 256], F32) for i in range(2)]
    kr_t = [arA.t(f"kr{i}", [128, 256], BF16) for i in range(2)]
    krd_t = [arA.t(f"krd{i}", [128, 512], BF16, f"krd{i}") for i in range(2)]
    kT_t = [arA.t(f"kTt{i}", [128, 512], BF16, f"kT{i}") for i in range(2)]
    v_t = [arA.t(f"vt{i}", [128, 260], BF16, f"v{i}") for i in range(2)]
    wqT_sb = arA.t("wqT_sb", [128, 8 * 1024], BF16)
    wkT_sb = arA.t("wkT_sb", [128, 8 * 256], BF16)
    wvT_sb = arA.t("wvT_sb", [128, 8 * 256], BF16)
    woT_sb = sb("woT_sb", [128, 8 * 1024], BF16)
    qt_sb = [arA.t(f"qt_sb{i}", [128, 1024], BF16, f"qts{i}") for i in range(2)]
    y65_sb = [arA.t(f"y65_sb{i}", [128, 1040], F32, f"y65{i}") for i in range(2)]
    ymix_sb = [arA.t(f"ymix_sb{i}", [128, 1024], BF16, f"ymix{i}") for i in range(2)]
    ymixT_sb = [arA.t(f"ymixT_sb{i}", [128, 1024], BF16, f"ymixT{i}")
                for i in range(2)]
    e_sb = [[arA.t(f"e_sb{i}_{j}", [128, 1536], BF16, f"e{i}{j}")
             for j in range(2)] for i in range(2)]
    t_w = arA.t("t_w", [128, QFD], F32)
    qlo_w = arA.t("qlo_w", [128, QFD], F32)
    bm_w = arA.t("bm_w", [128, QFD // 16], F32)
    sc_w = arA.t("sc_w", [128, QFD // 16], F32)
    sh_w = arA.t("sh_w", [128, QFD // 16], F32)
    r_w = arA.t("r_w", [128, QFD // 16], F32)
    wsq_sb = arA.t("wsq_sb", [128, 2816], BF16)
    wstg = arA.t("wstg", [128, 1408], F32)
    wtT_sb = arA.t("wtT_sb", [128, 1024], BF16)

    arB = Ar()
    wg_sb = arB.t("wg_sb", [128, 8 * IHALF], BF16)
    wu_sb = arB.t("wu_sb", [128, 8 * IHALF], BF16)
    wd_sb = arB.t("wd_sb", [128, NCH * 1024], BF16)
    xqs_sb = [arB.t(f"xqs{i}", [128, 1024], BF16) for i in range(2)]
    ain_sb = [arB.t(f"ain{i}", [128, 1024], F32) for i in range(2)]
    sg_sb = [arB.t(f"sg_sb{i}", [128, IHALF], F32) for i in range(2)]
    h_sb = [arB.t(f"h_sb{i}", [128, IHALF], F32) for i in range(4)]
    hq_sb = [arB.t(f"hq_sb{i}", [128, IHALF], BF16) for i in range(2)]
    hqT_sb = [arB.t(f"hqT_sb{i}", [128, IHALF], BF16) for i in range(2)]

    def dma(qe, out_ap, in_ap, reads, writes, tag=""):
        p.add(qe, lambda e, o=out_ap, i=in_ap: e.dma_start(o, i),
              reads=reads, writes=writes, dma=True, tag=tag)

    def quant_chain(src_ap, out_ap, fd, src_key, out_key, pr=128, deq_eng="dve",
                    scr="a"):
        nb = fd // 16
        if scr == "w":
            tS, qloS, q2S, dqS = t_w, qlo_w, t_w, qlo_w
            bmS, scS, shS, rS = bm_w, sc_w, sh_w, r_w
            kT, kQ = "t_w", "qlo_w"
            kB, kC, kH, kR = "bm_w", "sc_w", "sh_w", "r_w"
        else:
            tS, qloS, q2S, dqS = t_s, qlo_s, q2_s, dq_s
            bmS, scS, shS, rS = bm_s, sc_s, sh_s, r_s
            kT, kQ = "t_s", "qlo_s"
            kB, kC, kH, kR = "bm_s", "sc_s", "sh_s", "r_s"
        src3 = AP(src_ap.tensor, src_ap.offset,
                  [list(src_ap.ap[0]), [16, nb], [1, 16]])
        p.add("dve", lambda e, s=src3, nb=nb, pr=pr: e.tensor_reduce(
            bmS[0:pr, 0:nb], s, AX.X, ALU.max, apply_absolute_value=True),
            reads=(src_key,), writes=(kB,))
        p.add("dve", lambda e, nb=nb, pr=pr: e.tensor_scalar(
            scS[0:pr, 0:nb], bmS[0:pr, 0:nb], 1.0 / 6.0, 1e-12, ALU.mult, ALU.max),
            reads=(kB,), writes=(kC,))
        p.add("dve", lambda e, nb=nb, pr=pr: e.reciprocal(
            rS[0:pr, 0:nb], scS[0:pr, 0:nb]),
            reads=(kC,), writes=(kR,))
        p.add("dve", lambda e, nb=nb, pr=pr: e.tensor_scalar_mul(
            shS[0:pr, 0:nb], scS[0:pr, 0:nb], 0.5),
            reads=(kC,), writes=(kH,))
        rbc = AP(rS, 0, [[QFD // 16, pr], [1, nb], [0, 16]])
        sbc = AP(shS, 0, [[QFD // 16, pr], [1, nb], [0, 16]])
        p.add("dve", lambda e, s=src_ap, r=rbc, fd=fd, pr=pr: e._custom_dve(
            T_ABS, out=tS[0:pr, 0:fd], in0=s, in1=r),
            reads=(src_key, kR), writes=(kT,))
        p.add("dve", lambda e, fd=fd, pr=pr: e._custom_dve(
            SNAP_LO, out=qloS[0:pr, 0:fd], in0=tS[0:pr, 0:fd], in1=c175[0:pr, 0:1],
            s0=0.25, s1=0.75, imm2=1.25),
            reads=(kT, "c175"), writes=(kQ,))
        p.add("dve", lambda e, fd=fd, pr=pr: e._custom_dve(
            SNAP_HI, out=q2S[0:pr, 0:fd], in0=tS[0:pr, 0:fd], in1=qloS[0:pr, 0:fd],
            s0=2.5, s1=3.5, imm2=5.0),
            reads=(kT, kQ), writes=(kT,))
        p.add(deq_eng, lambda e, s=sbc, fd=fd, pr=pr: e.tensor_tensor(
            dqS[0:pr, 0:fd], q2S[0:pr, 0:fd], s, ALU.mult),
            reads=(kT, kH), writes=(kQ,))
        p.add("dve", lambda e, o=out_ap, g=src_ap, fd=fd, pr=pr: e._custom_dve(
            FINAL_SIGN, out=o, in0=dqS[0:pr, 0:fd], in1=g),
            reads=(kQ, src_key), writes=(out_key,))

    def mm_group(emits, reads, writes, tag=""):
        def f(e, emits=tuple(emits)):
            last = None
            for fn in emits:
                last = fn(e)
            return last
        p.add("pe", f, reads=reads, writes=writes, tag=tag)

    def transpose8(src, src_key, dst, dst_key, bank0, n, f32mode):
        step = 4
        for c4 in range(0, n, step):
            nch = min(step, n - c4)
            bank = bank0 + ((c4 // step) % 2)
            for c in range(c4, c4 + nch):
                if f32mode:
                    p.add("pe", lambda e, c=c, c4=c4, bank=bank, src=src: e.transpose(
                        pb[bank][:, (c - c4) * 128:(c - c4 + 1) * 128],
                        src[:, c * 128:(c + 1) * 128], ident_f[:, :]),
                        reads=(src_key, "ident_f"), writes=(PS[bank],))
                else:
                    p.add("pe", lambda e, c=c, c4=c4, bank=bank, src=src: e.transpose(
                        pbb[bank][:, (c - c4) * 128:(c - c4 + 1) * 128],
                        src[:, c * 128:(c + 1) * 128], ident_b[:, :]),
                        reads=(src_key, "ident_b"), writes=(PS[bank],))
            if f32mode:
                p.add("act", lambda e, c4=c4, nch=nch, bank=bank, dst=dst: e.copy(
                    dst[:, c4 * 128:(c4 + nch) * 128], pb[bank][:, 0:nch * 128]),
                    reads=(PS[bank],), writes=(dst_key,))
            else:
                p.add("act", lambda e, c4=c4, nch=nch, bank=bank, dst=dst: e.copy(
                    dst[:, c4 * 128:(c4 + nch) * 128], pbb[bank][:, 0:nch * 128]),
                    reads=(PS[bank],), writes=(dst_key,))

    p.add("dve", lambda e: e.memset(c175[:, :], 1.75), writes=("c175",))
    p.add("dve", lambda e: e.memset(eps5[:, :], 1e-5), writes=("eps5",))
    p.add("dve", lambda e: e.memset(eps6[:, :], 1e-6), writes=("eps6",))
    p.add("dve", lambda e: e.memset(lnh[:, :], -0.6931471805599453), writes=("lnh",))
    p.add("dve", lambda e: e.memset(ones1[:, :], 1.0), writes=("ones1",))
    dma("sp", ident_f[:, :], idf_in[:, :], (), ("ident_f",))
    dma("sp", ident_b[:, :], idb_in[:, :], (), ("ident_b",))
    dma("sp", AP(msk_sb, 0, [[3 * 384, 128], [384, 3], [1, 384]]),
        AP(msk_in, 0, [[384, 128], [128 * 384, 3], [1, 384]]), (), ("msk_sb",))
    dma("sp", row_sb[0:1, 0:H], qg_in[0:1, :], (), ("row_q",))
    dma("sp", row_sb[0:1, 32:64], pm_in[0:1, :], (), ("row_pm",))
    p.add("act", lambda e: e.activation(row_sb[0:1, 16:32], row_sb[0:1, 0:H], AF.Abs),
          reads=("row_q",), writes=("row_a",))
    p.add("dve", lambda e: e.tensor_scalar_mul(
        row_sb[0:1, 16:32], row_sb[0:1, 16:32], -8.0),
        reads=("row_a",), writes=("row_a",))
    p.add("dve", lambda e: e.tensor_scalar_mul(
        row_sb[0:1, 0:16], row_sb[0:1, 0:16], 0.125),
        reads=("row_q",), writes=("row_q2",))
    mm_group([lambda e: e.matmul(pb[6][:, 0:64], ones1[0:1, :], row_sb[0:1, 0:64],
                                 start=True, stop=True)],
             reads=("ones1", "row_a", "row_q2", "row_pm"), writes=(PS[6],))
    p.add("act", lambda e: e.copy(gb_sb[:, 0:16], pb[6][:, 16:32]),
          reads=(PS[6],), writes=("gb_sb",))
    p.add("act", lambda e: e.copy(gb_sb[:, 16:32], pb[6][:, 0:16]),
          reads=(PS[6],), writes=("gb_sb",))
    p.add("act", lambda e: e.copy(pm_sb[:, :], pb[6][:, 32:64]),
          reads=(PS[6],), writes=("pm_sb",))

    for (wdram, wsb, wkey, ncols) in ((wq_t, wqT_sb, "wqT_sb", 1024),
                                      (wk_t, wkT_sb, "wkT_sb", 256),
                                      (wv_t, wvT_sb, "wvT_sb", 256),
                                      (wo_t, woT_sb, "woT_sb", 1024)):
        for c in range(8):
            st = stage[c % 2]
            dma("sp", st[:, 0:ncols], wdram[c * 128:(c + 1) * 128, :],
                (f"stg{c % 2}",), (f"stg{c % 2}",), tag="wcast_in")
            p.add("act", lambda e, st=st, wsb=wsb, c=c, ncols=ncols: e.copy(
                wsb[:, c * ncols:(c + 1) * ncols], st[:, 0:ncols]),
                reads=(f"stg{c % 2}",), writes=(wkey,))

    def quant_weight_full(dram_in, my_out, out_key):
        rows, cols = dram_in.shape
        for r in range(rows // 128):
            for h0 in range(0, cols, IHALF):
                fd = min(IHALF, cols - h0)
                if cols == D:
                    early = r < (rows // 128) // 2
                else:
                    early = h0 == 0
                dma("sp", wstg[:, 0:fd],
                    dram_in[r * 128:(r + 1) * 128, h0:h0 + fd],
                    ("wstg",), ("wstg",),
                    tag="wqheadE" if early else "wqheadL")
                quant_chain(wstg[:, 0:fd], wsq_sb[:, h0:h0 + fd],
                            fd, "wstg", "wsq_sb", deq_eng="pool", scr="w")
                for c4 in range(h0 // 128, (h0 + fd) // 128, 2):
                    nch = min(2, (h0 + fd) // 128 - c4)
                    bank = 2 + (c4 // 2) % 2
                    for c in range(c4, c4 + nch):
                        p.add("pe", lambda e, c=c, c4=c4, bank=bank: e.transpose(
                            pbb[bank][:, 768 + (c - c4) * 128:
                                      768 + (c - c4 + 1) * 128],
                            wsq_sb[:, c * 128:(c + 1) * 128], ident_b[:, :]),
                            reads=("wsq_sb", "ident_b"), writes=(PS[bank],))
                    reg = ((c4 // 2) % 4) * 256
                    p.add("act", lambda e, nch=nch, bank=bank, reg=reg: e.copy(
                        wtT_sb[:, reg:reg + nch * 128],
                        pbb[bank][:, 768:768 + nch * 128]),
                        reads=(PS[bank],), writes=(f"wtT:{(c4 // 2) % 4}",))
                    dma("pool",
                        AP(my_out, c4 * 128 * rows + r * 128,
                           [[rows, 128], [128 * rows, nch], [1, 128]]),
                        AP(wtT_sb, reg, [[1024, 128], [128, nch], [1, 128]]),
                        (f"wtT:{(c4 // 2) % 4}",), (f"{out_key}:{r}:{c4}",),
                        tag="wsh_out")

    wq_start = len(p.ops)
    quant_weight_full(wg_in, wgq_d, "wgq_d")
    quant_weight_full(wu_in, wuq_d, "wuq_d")
    quant_weight_full(wd_in, wdq_d, "wdq_d")
    wq_end = len(p.ops)
    def wgqkeys(hf):
        return tuple(f"wgq_d:{r}:{c4}" for r in range(hf * 11, hf * 11 + 11)
                     for c4 in range(0, D // 128, 2))

    def wuqkeys(hf):
        return tuple(f"wuq_d:{r}:{c4}" for r in range(hf * 11, hf * 11 + 11)
                     for c4 in range(0, D // 128, 2))

    def wdqkeys(hf):
        return tuple(f"wdq_d:{r}:{c4}" for r in range(D // 128)
                     for c4 in range(hf * 11, hf * 11 + 11, 2))

    def rope(src, src_key, dst, dst_key, nh, b3):
        def v(t, off):
            return AP(t, off, [[t.shape[1], 128], [64, nh], [1, 32]])

        def ctab(t):
            return AP(t, 0, [[32, 128], [0, nh], [1, 32]])
        s1a, s2a = v(src, 0), v(src, 32)
        d1a, d2a = v(dst, 0), v(dst, 32)
        cr, sr = ctab(cosr[b3]), ctab(sinr[b3])
        rt1v = AP(rt1, 0, [[512, 128], [32, nh], [1, 32]])
        rt2v = AP(rt2, 0, [[512, 128], [32, nh], [1, 32]])
        p.add("dve", lambda e: e.tensor_tensor(rt1v, s1a, cr, ALU.mult),
              reads=(src_key, f"cos{b3}"), writes=("rt1",))
        p.add("dve", lambda e: e.tensor_tensor(rt2v, s2a, sr, ALU.mult),
              reads=(src_key, f"sin{b3}"), writes=("rt2",))
        p.add("dve", lambda e: e.tensor_tensor(d1a, rt1v, rt2v, ALU.add),
              reads=("rt1", "rt2"), writes=(dst_key,))
        p.add("dve", lambda e: e.tensor_tensor(rt1v, s2a, cr, ALU.mult),
              reads=(src_key, f"cos{b3}"), writes=("rt1",))
        p.add("dve", lambda e: e.tensor_tensor(rt2v, s1a, sr, ALU.mult),
              reads=(src_key, f"sin{b3}"), writes=("rt2",))
        p.add("dve", lambda e: e.tensor_tensor(d2a, rt1v, rt2v, ALU.subtract),
              reads=("rt1", "rt2"), writes=(dst_key,))

    def p1_stage_a(it):
        bf = it % 2
        b3 = it % 3
        own = it >= 2
        xs = stage[bf]
        dma("sp", xs[:, :], x_in[it * 128:(it + 1) * 128, :],
            (f"stg{bf}",), (f"stg{bf}",), tag=f"x{it}")
        b4 = it % 4
        dma("sp", cosr[b4][:, :], cos_in[it * 128:(it + 1) * 128, :],
            (f"cos{b4}",), (f"cos{b4}",))
        dma("sp", sinr[b4][:, :], sin_in[it * 128:(it + 1) * 128, :],
            (f"sin{b4}",), (f"sin{b4}",))
        p.add("dve", lambda e: e.bn_stats(bn_s[:, 0:6], xs[:, 0:512]),
              reads=(f"stg{bf}",), writes=("bn_s",))
        p.add("dve", lambda e: e.bn_stats(bn_s[:, 6:12], xs[:, 512:1024]),
              reads=(f"stg{bf}",), writes=("bn_s",))
        p.add("dve", lambda e: e.bn_aggr(mv_s[:, :], bn_s[:, :]),
              reads=("bn_s",), writes=("mv_s",))
        p.add("act", lambda e: e.activation(r1_s[:, :], mv_s[:, 1:2], AF.Ln,
                                            bias=eps5[:, 0:1], scale=1.0),
              reads=("mv_s", "eps5"), writes=("r1_s",))
        p.add("act", lambda e: e.activation(r2_s[:, :], r1_s[:, :], AF.Exp,
                                            bias=0.0, scale=-0.5),
              reads=("r1_s",), writes=("r2_s",))
        p.add("dve", lambda e: e.tensor_scalar(nm_s[:, :], mv_s[:, 0:1],
                                               r2_s[:, 0:1], -1.0,
                                               ALU.mult, ALU.mult),
              reads=("mv_s", "r2_s"), writes=("nm_s",))
        p.add("act", lambda e: e.activation(xn_t[bf][:, :], xs[:, :], AF.Identity,
                                            bias=nm_s[:, 0:1], scale=r2_s[:, 0:1]),
              reads=(f"stg{bf}", "nm_s", "r2_s"), writes=(f"xn{bf}",))
        transpose8(xn_t[bf], f"xn{bf}", xnT_t[b3], f"xnT{b3}", 0, 8, True)
        if own:
            quant_chain(xn_t[bf][:, :], xq_t[bf][:, :], 1024, f"xn{bf}",
                        f"xq{bf}", deq_eng="dve")
            if _DEBUG:
                dma("pool", xq_dbg[(it - 2) * 128:(it - 1) * 128, :],
                    xq_t[bf][:, :], (f"xq{bf}",), (f"xqdbg:{it}",))

    def p1_stage_a2(it):
        bf = it % 2
        transpose8(xq_t[bf], f"xq{bf}", xqT_t[bf], f"xqT{bf}", 2, 8, False)
        dma("pool",
            AP(xqT_d, (it - 2) * 128, [[TO, 128], [128 * TO, 8], [1, 128]]),
            AP(xqT_t[bf], 0, [[1024, 128], [128, 8], [1, 128]]),
            (f"xqT{bf}",), (f"xqT_d:{it - 2}",), tag="xqT_out")

    def p1_stage_b1(it):
        bf = it % 2
        b3 = it % 3
        own = it >= 2

        def kv_mms(wsb, wkey, bank):
            ems = []
            for c in range(8):
                ems.append(lambda e, c=c, wsb=wsb, bank=bank: e.matmul(
                    pb[bank][:, 0:256], xnT_t[b3][:, c * 128:(c + 1) * 128],
                    wsb[:, c * 256:(c + 1) * 256],
                    start=(c == 0), stop=(c == 7)))
            mm_group(ems, reads=(f"xnT{b3}", wkey), writes=(PS[bank],), tag="kvmm")

        kv_mms(wkT_sb, "wkT_sb", 6)
        if own:
            for half in range(2):
                ems = []
                for c in range(8):
                    ems.append(lambda e, c=c, half=half: e.matmul(
                        pb[4 + half][:, 0:512],
                        xnT_t[b3][:, c * 128:(c + 1) * 128],
                        wqT_sb[:, c * 1024 + half * 512: c * 1024 + half * 512 + 512],
                        start=(c == 0), stop=(c == 7)))
                mm_group(ems, reads=(f"xnT{b3}", "wqT_sb"),
                         writes=(PS[4 + half],), tag="qmm")
        p.add("act", lambda e, bf=bf: e.activation(
            sq_t[bf][:, 1024:1280], AP(pb[6], 0, [[512, 128], [1, 256]]),
            AF.Square), reads=(PS[6],), writes=(f"sq{bf}",))
        if own:
            for half in range(2):
                p.add("act", lambda e, half=half, bf=bf: e.activation(
                    sq_t[bf][:, half * 512:(half + 1) * 512],
                    AP(pb[4 + half], 0, [[512, 128], [1, 512]]), AF.Square),
                    reads=(PS[4 + half],), writes=(f"sq{bf}",))
        p.add("act", lambda e, bf=bf: e.copy(
            kn_t[bf][:, :], AP(pb[6], 0, [[512, 128], [1, 256]])),
            reads=(PS[6],), writes=(f"kn{bf}",))
        if own:
            for half in range(2):
                p.add("act", lambda e, half=half, bf=bf: e.copy(
                    AP(qn_t[bf], half * 512, [[1024, 128], [1, 512]]),
                    AP(pb[4 + half], 0, [[512, 128], [1, 512]])),
                    reads=(PS[4 + half],), writes=(f"qn{bf}",))
        kv_mms(wvT_sb, "wvT_sb", 7)
        p.add("act", lambda e: e.copy(
            AP(v_t[bf], 0, [[260, 128], [65, 4], [1, 64]]),
            AP(pb[7], 0, [[512, 128], [64, 4], [1, 64]])),
            reads=(PS[7], f"v{bf}"), writes=(f"v{bf}",))
        dma("pool", v_d[it, :, :], v_t[bf][:, :], (f"v{bf}",), (f"v_d:{it}",))

    def p1_stage_b2_head(it):
        bf = it % 2
        own = it >= 2
        nh_tot = 20 if own else 4
        sq3 = AP(sq_t[bf], 1024 if not own else 0,
                 [[1280, 128], [64, nh_tot], [1, 64]])
        p.add("dve", lambda e, sq3=sq3, nh=nh_tot: e.tensor_reduce(
            rk_s[:, 0:nh], sq3, AX.X, ALU.add),
            reads=(f"sq{bf}",), writes=("rk_s",))
        p.add("act", lambda e, nh=nh_tot: e.activation(
            rk2_s[:, 0:nh], rk_s[:, 0:nh], AF.Ln, bias=eps6[:, 0:1], scale=1.0 / 64),
            reads=("rk_s", "eps6"), writes=("rk2_s",))
        p.add("act", lambda e, nh=nh_tot: e.activation(
            rk_s[:, 0:nh], rk2_s[:, 0:nh], AF.Exp, bias=0.0, scale=-0.5),
            reads=("rk2_s",), writes=("rk_s",))

    def p1_stage_b2(it):
        bf = it % 2
        b4 = it % 4
        own = it >= 2
        koff = 16 if own else 0
        rkbc = AP(rk_s, koff, [[20, 128], [1, 4], [0, 64]])
        p.add("dve", lambda e, bf=bf, rkbc=rkbc: e.tensor_tensor(
            kn_t[bf][:, :], kn_t[bf][:, :], rkbc, ALU.mult),
            reads=(f"kn{bf}", "rk_s"), writes=(f"kn{bf}",))
        rope(kn_t[bf], f"kn{bf}", kr_t[bf], f"kr{bf}", 4, b4)
        p.add("dve", lambda e: e.tensor_copy(
            AP(krd_t[bf], 0, [[512, 128], [128, 4], [64, 2], [1, 64]]),
            AP(kr_t[bf], 0, [[256, 128], [64, 4], [0, 2], [1, 64]])),
            reads=(f"kr{bf}",), writes=(f"krd{bf}",))
        if own:
            rbc = AP(rk_s, 0, [[20, 128], [1, 16], [0, 64]])
            p.add("dve", lambda e, bf=bf, rbc=rbc: e.tensor_tensor(
                AP(qn_t[bf], 0, [[1024, 128], [1, 1024]]),
                AP(qn_t[bf], 0, [[1024, 128], [1, 1024]]),
                rbc, ALU.mult),
                reads=(f"qn{bf}", "rk_s"), writes=(f"qn{bf}",))
            rope(qn_t[bf], f"qn{bf}", qr_t[bf], f"qr{bf}", 16, b4)

    def p1_stage_b3(it):
        bf = it % 2
        own = it >= 2
        for c in range(4):
            p.add("pe", lambda e, c=c: e.transpose(
                pbb[3][:, 0:128], krd_t[bf][:, c * 128:(c + 1) * 128],
                ident_b[:, :]),
                reads=(f"krd{bf}", "ident_b"), writes=(PS[3],))
            p.add("act", lambda e, c=c: e.copy(
                kT_t[bf][:, c * 128:(c + 1) * 128], pbb[3][:, 0:128]),
                reads=(PS[3],), writes=(f"kT{bf}",))
        dma("pool",
            AP(kT_d, it * 128, [[TT, 128], [128 * TT, 4], [1, 128]]),
            AP(kT_t[bf], 0, [[512, 128], [128, 4], [1, 128]]),
            (f"kT{bf}",), (f"kT_d:{it}",))
        if own:
            transpose8(qr_t[bf], f"qr{bf}", qT_t[bf], f"qT{bf}", 0, 8, False)
            dma("pool",
                AP(qT_d, (it - 2) * 128, [[TO, 128], [128 * TO, 8], [1, 128]]),
                AP(qT_t[bf], 0, [[1024, 128], [128, 8], [1, 128]]),
                (f"qT{bf}",), (f"qT_d:{it - 2}",), tag="qT_out")

    def p2_stage_a(qi):
        bf = qi % 2
        it = qi + 2
        dma("sp", AP(qt_sb[bf], 0, [[1024, 128], [128, 8], [1, 128]]),
            AP(qT_d, qi * 128, [[TO, 128], [128 * TO, 8], [1, 128]]),
            (f"qT_d:{qi}", f"qts{bf}"), (f"qts{bf}",), tag="qt_in")
        dma("sp", AP(kt_sb[bf], 0, [[1536, 128], [384, 4], [1, 384]]),
            AP(kT_d, qi * 128, [[TT, 128], [128 * TT, 4], [1, 384]]),
            tuple([f"kT_d:{j}" for j in range(qi, qi + 3)]) + (f"kts{bf}",),
            (f"kts{bf}",), tag="kt_in")
        dma("sp", AP(v2_sb[bf], 0, [[780, 128], [260, 3], [1, 260]]),
            AP(v_d, qi * 128 * 260, [[260, 128], [128 * 260, 3], [1, 260]]),
            tuple([f"v_d:{j}" for j in range(qi, qi + 3)]) + (f"v2s{bf}",),
            (f"v2s{bf}",), tag="v_in")
        dma("sp", xres_sb[bf][:, :], x_in[it * 128:(it + 1) * 128, :],
            (f"xres{bf}",), (f"xres{bf}",), tag="xres")
        if _DEBUG and qi == 1:
            dma("pool", qk_dbg[0, :, 0:1024], qt_sb[bf][:, :], (f"qts{bf}",), ("qkdbg0",))
            dma("pool", qk_dbg[1, :, :], kt_sb[bf][:, :], (f"kts{bf}",), ("qkdbg1",))

    def p2_scores(qi, g):
        bf = qi % 2
        sel = min(qi, 2)
        if True:
            for k2 in range(2):
                m = 2 * g + k2
                a, b2 = PAIRS[m]
                sb0 = (m % 2) * 2
                kv = a // 4
                for c in range(3):
                    def em(e, c=c, m=m, sb0=sb0, kv=kv, bf=bf):
                        e.matmul(pb[sb0][:, c * 128:(c + 1) * 128],
                                 kt_sb[bf][0:64, kv * 384 + c * 128:
                                           kv * 384 + (c + 1) * 128],
                                 qt_sb[bf][0:64, m * 128:(m + 1) * 128],
                                 start=True, stop=True, tile_position=(0, 0))
                        return e.matmul(
                            pb[sb0 + 1][:, c * 128:(c + 1) * 128],
                            kt_sb[bf][64:128, kv * 384 + c * 128:
                                      kv * 384 + (c + 1) * 128],
                            qt_sb[bf][64:128, m * 128:(m + 1) * 128],
                            start=True, stop=True, tile_position=(64, 0))
                    p.add("pe", em, reads=(f"kts{bf}", f"qts{bf}"),
                          writes=(PS[sb0], PS[sb0 + 1]), tag="score")
                for j, hh in enumerate((a, b2)):
                    s = 2 * k2 + j
                    p.add("act", lambda e, s=s, hh=hh, bk=sb0 + j, bf=bf, g=g:
                          e.activation(
                              e_sb[bf][g % 2][:, s * 384:(s + 1) * 384],
                              AP(pb[bk], 0, [[512, 128], [1, 384]]),
                              AF.Exp, bias=gb_sb[:, hh:hh + 1],
                              scale=gb_sb[:, 16 + hh:17 + hh]),
                          reads=(PS[sb0 + j], "gb_sb"), writes=(f"e{bf}{g % 2}",))
            p.add("dve", lambda e, sel=sel, bf=bf, g=g: e.tensor_tensor(
                AP(e_sb[bf][g % 2], 0, [[1536, 128], [384, 4], [1, 384]]),
                AP(e_sb[bf][g % 2], 0, [[1536, 128], [384, 4], [1, 384]]),
                AP(msk_sb, sel * 384, [[3 * 384, 128], [0, 4], [1, 384]]),
                ALU.mult),
                reads=(f"e{bf}{g % 2}", "msk_sb"), writes=(f"e{bf}{g % 2}",),
                tag="mask")
            if _DEBUG and qi == 1:
                dma("pool", e2_dbg[g, :, :], e_sb[bf][g % 2][:, :],
                    (f"e{bf}{g % 2}",), (f"e2dbg:{g}",), tag="e2dbg")

    def p2_y(qi, g):
        bf = qi % 2
        ybank = 4 + (g % 2)
        if True:
            for k2 in range(2):
                m = 2 * g + k2
                a, b2 = PAIRS[m]
                for j, hh in enumerate((a, b2)):
                    s = 2 * k2 + j
                    kv = hh // 4
                    ems = []
                    for c in range(3):
                        ems.append(lambda e, c=c, s=s, kv=kv, yb=ybank, bf=bf,
                                   g=g: e.matmul(
                                       pb[yb][:, s * 65:(s + 1) * 65],
                                       e_sb[bf][g % 2][:, s * 384 + c * 128:
                                                       s * 384 + (c + 1) * 128],
                                       v2_sb[bf][:, c * 260 + kv * 65:
                                                 c * 260 + (kv + 1) * 65],
                                       start=(c == 0), stop=(c == 2)))
                    mm_group(ems, reads=(f"e{bf}{g % 2}", f"v2s{bf}"),
                             writes=(PS[ybank],), tag="ymm")
            p.add("act", lambda e, g=g, yb=ybank, bf=bf: e.copy(
                AP(y65_sb[bf], 4 * g * 65, [[1040, 128], [1, 260]]),
                AP(pb[yb], 0, [[512, 128], [1, 260]])),
                reads=(PS[ybank],), writes=(f"y65{bf}",))

    def p2_stage_c(qi):
        bf = qi % 2
        p.add("dve", lambda e, bf=bf: e.reciprocal(
            rp_sb[:, :], AP(y65_sb[bf], 64, [[1040, 128], [65, 16]])),
            reads=(f"y65{bf}",), writes=("rp_sb",))
        p.add("dve", lambda e: e.tensor_tensor(
            sall_sb[:, :], AP(rp_sb, 0, [[16, 128], [2, 8], [0, 2], [1, 2]]),
            pm_sb[:, :], ALU.mult),
            reads=("rp_sb", "pm_sb"), writes=("sall_sb",))
        for pp in range(8):
            for o in range(2):
                p.add("dve", lambda e, pp=pp, o=o, bf=bf: e._custom_dve(
                    PMIX,
                    out=ymix_sb[bf][:, (2 * pp + o) * 64:(2 * pp + o + 1) * 64],
                    in0=AP(y65_sb[bf], (2 * pp) * 65, [[1040, 128], [1, 64]]),
                    in1=AP(y65_sb[bf], (2 * pp + 1) * 65, [[1040, 128], [1, 64]]),
                    s0=sall_sb[:, pp * 4 + o * 2: pp * 4 + o * 2 + 1],
                    s1=sall_sb[:, pp * 4 + o * 2 + 1: pp * 4 + o * 2 + 2]),
                    reads=(f"y65{bf}", "sall_sb"), writes=(f"ymix{bf}",),
                    tag="pmix")
        if _DEBUG:
            dma("pool", ym_dbg[qi * 128:(qi + 1) * 128, :], ymix_sb[bf][:, :],
                (f"ymix{bf}",), (f"ymdbg:{qi}",))

    def p2_stage_c1b(qi):
        bf = qi % 2
        transpose8(ymix_sb[bf], f"ymix{bf}", ymixT_sb[bf], f"ymixT{bf}", 0, 8, False)
        for half in range(2):
            ems = []
            for c in range(8):
                ems.append(lambda e, c=c, half=half, bf=bf: e.matmul(
                    pb[6 + half][:, 0:512],
                    ymixT_sb[bf][:, c * 128:(c + 1) * 128],
                    woT_sb[:, c * 1024 + half * 512: c * 1024 + half * 512 + 512],
                    start=(c == 0), stop=(c == 7)))
            mm_group(ems, reads=(f"ymixT{bf}", "woT_sb"),
                     writes=(PS[6 + half],), tag="omm")

    def p2_stage_c2(qi):
        bf = qi % 2
        for half in range(2):
            p.add("dve", lambda e, half=half, bf=bf: e.tensor_tensor(
                ostage[bf][:, half * 512:(half + 1) * 512],
                AP(pb[6 + half], 0, [[512, 128], [1, 512]]),
                xres_sb[bf][:, half * 512:(half + 1) * 512], ALU.add),
                reads=(PS[6 + half], f"xres{bf}"), writes=(f"ost{bf}",))
        dma("pool", acc_d[qi * 128:(qi + 1) * 128, :], ostage[bf][:, :],
            (f"ost{bf}",), (f"acc_d:{qi}",), tag="acc_out")

    def emit_p2_tile(qi, pending_c):
        p2_scores(qi, 0)
        for g in range(4):
            if g + 1 < 4:
                p2_scores(qi, g + 1)
            if g == 2 and pending_c:
                p2_stage_c1b(pending_c[-1])
            p2_y(qi, g)
            if g == 0 and pending_c:
                p2_stage_c(pending_c[-1])
        if pending_c:
            p2_stage_c2(pending_c.pop())
        if qi + 1 < NQ:
            p2_stage_a(qi + 1)
        pending_c.append(qi)
        p.add("sp", lambda e: e.nop(), tag="anchor3")

    pending_c = []
    for i in range(2):
        p.add("dve", lambda e, i=i: e.memset(v_t[i][:, :], 1.0),
              reads=(), writes=(f"v{i}",))
    p1_stage_a(0)
    p1_stage_a(1)
    for it in range(NT):
        if it >= 2:
            p1_stage_b3(it - 2)
        if 2 <= it + 1 < NT:
            p1_stage_a2(it + 1)
        if it + 2 < NT:
            p1_stage_a(it + 2)
        if it >= 1:
            p1_stage_b2_head(it - 1)
        p1_stage_b1(it)
        if it >= 1:
            p1_stage_b2(it - 1)
        p.add("sp", lambda e: e.nop(), tag="anchor1")
        qi = it - 5
        if qi == 0:
            p2_stage_a(0)
        if 0 <= qi < NQ:
            emit_p2_tile(qi, pending_c)
    p1_stage_b2_head(NT - 1)
    p1_stage_b2(NT - 1)
    p1_stage_b3(NT - 2)
    p1_stage_b3(NT - 1)

    for qi in range(NT - 5, NQ):
        emit_p2_tile(qi, pending_c)
    last = pending_c.pop()
    p2_stage_c(last)
    p2_stage_c1b(last)
    p2_stage_c2(last)

    p.add("sp", lambda e: e.nop(), reads=tuple(arA.keys), writes=("arena_free",),
          tag="barrier")

    def build_ffn_weights(hf):
        for c in range(8):
            dma("sp", wg_sb[:, c * IHALF:(c + 1) * IHALF],
                wgq_d[c * 128:(c + 1) * 128, hf * IHALF:(hf + 1) * IHALF],
                wgqkeys(hf) + ("arena_free", f"wg:{c}"), (f"wg:{c}",),
                tag="wgin" if hf == 0 else "wgin2")
            dma("sp", wu_sb[:, c * IHALF:(c + 1) * IHALF],
                wuq_d[c * 128:(c + 1) * 128, hf * IHALF:(hf + 1) * IHALF],
                wuqkeys(hf) + ("arena_free", f"wu:{c}"), (f"wu:{c}",),
                tag="wuin")
        for ch in range(NCH):
            dma("sp", wd_sb[:, ch * 1024:(ch + 1) * 1024],
                wdq_d[hf * IHALF + ch * 128: hf * IHALF + (ch + 1) * 128, :],
                wdqkeys(hf) + ("arena_free", f"wd:{ch}"), (f"wd:{ch}",),
                tag="wdin")

    WGKEYS = tuple(f"wg:{c}" for c in range(8))
    WUKEYS = tuple(f"wu:{c}" for c in range(8))
    WDKEYS = tuple(f"wd:{ch}" for ch in range(NCH))
    REG = ((0, 0, 512), (1, 512, 1024), (2, 1024, 1408))

    def ffn_stage_a(hf, qi):
        bf = qi % 2
        hb = qi % 4
        dma("sp", AP(xqs_sb[bf], 0, [[1024, 128], [128, 8], [1, 128]]),
            AP(xqT_d, qi * 128, [[TO, 128], [128 * TO, 8], [1, 128]]),
            (f"xqT_d:{qi}", "arena_free", f"xqs{bf}"), (f"xqs{bf}",), tag="xq_in")
        for (wsb, wkeys, b0) in ((wg_sb, WGKEYS, 0), (wu_sb, WUKEYS, 3)):
            for (ri, c0, c1) in REG:
                ems = []
                for c in range(8):
                    ems.append(lambda e, c=c, c0=c0, c1=c1, ri=ri, wsb=wsb,
                               b0=b0, bf=bf: e.matmul(
                        pb[b0 + ri][:, 0:c1 - c0],
                        xqs_sb[bf][:, c * 128:(c + 1) * 128],
                        wsb[:, c * IHALF + c0: c * IHALF + c1],
                        start=(c == 0), stop=(c == 7)))
                mm_group(ems, reads=(f"xqs{bf}",) + wkeys,
                         writes=(PS[b0 + ri],), tag="gumm")
        for (ri, c0, c1) in REG:
            p.add("act", lambda e, ri=ri, c0=c0, c1=c1, bf=bf: e.activation(
                sg_sb[bf][:, c0:c1], AP(pb[ri], 0, [[512, 128], [1, c1 - c0]]),
                AF.Silu),
                reads=(PS[ri],), writes=(f"sg_sb{bf}",))

    def ffn_stage_h(hf, qi):
        bf = qi % 2
        hb = qi % 4
        for (ri, c0, c1) in REG:
            p.add("dve", lambda e, ri=ri, c0=c0, c1=c1, bf=bf, hb=hb:
                  e.tensor_tensor(
                      h_sb[hb][:, c0:c1], sg_sb[bf][:, c0:c1],
                      AP(pb[3 + ri], 0, [[512, 128], [1, c1 - c0]]), ALU.mult),
                  reads=(f"sg_sb{bf}", PS[3 + ri]), writes=(f"h_sb{hb}",))

    def ffn_stage_b(hf, qi):
        bf = qi % 2
        hb = qi % 4
        accsrc = acc_d if hf == 0 else acc2_d
        acckey = "acc_d" if hf == 0 else "acc2_d"
        dma("sp", ain_sb[bf][:, :], accsrc[qi * 128:(qi + 1) * 128, :],
            (f"{acckey}:{qi}", "arena_free", f"ain{bf}"), (f"ain{bf}",),
            tag="acc_in")
        quant_chain(h_sb[hb][:, 0:IHALF], hq_sb[bf][:, 0:IHALF], IHALF,
                    f"h_sb{hb}", f"hq_sb{bf}")
        transpose8(hq_sb[bf], f"hq_sb{bf}", hqT_sb[bf], f"hqT_sb{bf}", 6, NCH,
                   False)
        for half in range(2):
            ems = []
            for c in range(NCH):
                ems.append(lambda e, c=c, half=half, bf=bf: e.matmul(
                    pb[6 + half][:, 0:512],
                    hqT_sb[bf][:, c * 128:(c + 1) * 128],
                    wd_sb[:, c * 1024 + half * 512: c * 1024 + half * 512 + 512],
                    start=(c == 0), stop=(c == NCH - 1)))
            mm_group(ems, reads=(f"hqT_sb{bf}",) + WDKEYS,
                     writes=(PS[6 + half],), tag="dmm")
            p.add("dve", lambda e, half=half, bf=bf: e.tensor_tensor(
                ostage[bf][:, half * 512:(half + 1) * 512],
                AP(pb[6 + half], 0, [[512, 128], [1, 512]]),
                ain_sb[bf][:, half * 512:(half + 1) * 512], ALU.add),
                reads=(PS[6 + half], f"ain{bf}"), writes=(f"ost{bf}",))
        dst = acc2_d if hf == 0 else y_out
        dkey = f"acc2_d:{qi}" if hf == 0 else f"out:{qi}"
        dma("pool", dst[qi * 128:(qi + 1) * 128, :], ostage[bf][:, :],
            (f"ost{bf}",), (dkey,), tag="final_out")

    for hf in range(2):
        build_ffn_weights(hf)
        for j in range(2):
            ffn_stage_a(hf, j)
            ffn_stage_h(hf, j)
        for qi in range(NQ):
            if qi + 2 < NQ:
                ffn_stage_a(hf, qi + 2)
            ffn_stage_b(hf, qi)
            if qi + 2 < NQ:
                ffn_stage_h(hf, qi + 2)

    p.add("sp", lambda e: e.nop(), reads=tuple(f"out:{qi}" for qi in range(NQ)))

    wq_ops = p.ops[wq_start:wq_end]
    rest = p.ops[:wq_start] + p.ops[wq_end:]
    heads = [i for i, op in enumerate(wq_ops)
             if op["tag"] in ("wqheadE", "wqheadL")]
    heads.append(len(wq_ops))
    chunks = [wq_ops[heads[i]:heads[i + 1]] for i in range(len(heads) - 1)]
    early_chunks = [ch for ch in chunks if ch[0]["tag"] == "wqheadE"]
    late_chunks = [ch for ch in chunks if ch[0]["tag"] == "wqheadL"]
    FINE = True
    flat = [op for ch in chunks for op in ch]
    if not FINE:
        out, ci = [], 0
        last_anchor = max(i for i, op in enumerate(rest) if op["tag"] in
                          ("anchor1", "anchor3"))
        for i, op in enumerate(rest):
            out.append(op)
            if op["tag"] in ("anchor1", "anchor3") and ci < len(chunks):
                per = 1 if op["tag"] == "anchor1" else 2
                if i == last_anchor:
                    per = len(chunks) - ci
                for ch in chunks[ci:ci + per]:
                    out.extend(ch)
                ci += per
        fin = out.pop(next(i for i, op in enumerate(out) if op is p.ops[-1]))
        out.append(fin)
        p.ops = out
        return _finish(nc, p)
    n_dve = sum(1 for op in rest if op["e"] == "dve"
                and op["tag"] not in ("wqhead",))
    first_b = next(i for i, op in enumerate(rest) if op["tag"] == "wgin")
    n_elig = sum(1 for op in rest[:first_b] if op["e"] == "dve")
    rate = len(flat) / max(1, n_elig)
    out, wi, acc = [], 0, 0.0
    for i, op in enumerate(rest):
        if i == first_b:
            out.extend(flat[wi:])
            wi = len(flat)
        if i < first_b and op["e"] == "dve" and wi < len(flat):
            acc += rate
            while acc >= 1.0 and wi < len(flat):
                out.append(flat[wi])
                wi += 1
                acc -= 1.0
        out.append(op)
    if wi < len(flat):
        out.extend(flat[wi:])
    fin = out.pop(next(i for i, op in enumerate(out) if op is p.ops[-1]))
    out.append(fin)
    p.ops = out
    return _finish(nc, p)


def _finish(nc, p):
    p.schedule()
    es = contextlib.ExitStack()
    sems = {}
    for e in Prog.ENGINES:
        sems[("c", e)] = es.enter_context(nc.semaphore(f"cs_{e}"))
    for qn in Prog.QUEUES:
        sems[("d", qn)] = es.enter_context(nc.semaphore(f"ds_{qn}"))
    block = es.enter_context(nc.Block())

    def emit_engine(eng_obj, ename):
        for op in p.ops:
            if op["e"] != ename:
                continue
            for (key, val) in op["waits"]:
                eng_obj.wait_ge(sems[key], val)
            inst = op["f"](eng_obj)
            if op["inc"]:
                if op["dma"]:
                    inst.then_inc(sems[("d", op["q"])], 16)
                else:
                    inst.then_inc(sems[("c", ename)], 1)

    block.tensor(lambda e: emit_engine(e, "pe"))
    block.vector(lambda e: emit_engine(e, "dve"))
    block.scalar(lambda e: emit_engine(e, "act"))
    block.gpsimd(lambda e: emit_engine(e, "pool"))
    block.sync(lambda e: emit_engine(e, "sp"))
    es.close()
    mybir.codegen_inst_isa_subclasses(nc)
    return nc


_DEBUG = False
_NC_CACHE = None


def _get_nc():
    global _NC_CACHE
    if _NC_CACHE is None:
        _NC_CACHE = _build_program()
    return _NC_CACHE


def _rope_tables():
    inv = 1.0 / (10000.0 ** (np.arange(0, HD, 2, dtype=np.float32) / HD))
    t = np.arange(S, dtype=np.float32)
    fr = np.outer(t, inv)
    return np.cos(fr).astype(np.float32), np.sin(fr).astype(np.float32)


def _masks(seqstart):
    r = np.arange(128)
    m0 = (r[None, :] < r[:, None]).astype(np.float32)
    m2 = (r[None, :] >= r[:, None]).astype(np.float32)
    ones = np.ones((128, 128), np.float32)
    zeros = np.zeros((128, 128), np.float32)
    normal = np.concatenate([m0, ones, m2], axis=1)
    if not seqstart:
        out = np.stack([normal, normal, normal])
    else:
        t0 = np.concatenate([zeros, zeros, m2], axis=1)
        t1 = np.concatenate([zeros, ones, m2], axis=1)
        out = np.stack([t0, t1, normal])
    return out.astype(ml_dtypes.bfloat16)


def kernel(L0_attn_x, L0_attn_Wq, L0_attn_Wk, L0_attn_Wv, L0_attn_Wo,
           L0_attn_q_gain, L0_attn_pair_mix, L0_ffn_w_gate, L0_ffn_w_up,
           L0_ffn_w_down):
    x = np.ascontiguousarray(np.asarray(L0_attn_x, np.float32))
    cos_f, sin_f = _rope_tables()
    nc = _get_nc()
    in_maps = []
    for c in range(NCORE):
        b, q = c // 4, c % 4
        own = x[b, q * TO:(q + 1) * TO]
        halo = (np.zeros((HALO, D), np.float32) if q == 0
                else x[b, q * TO - HALO: q * TO])
        p0 = q * TO - HALO
        pos = np.clip(np.arange(p0, p0 + TT), 0, S - 1)
        m = dict(
            x_in=np.concatenate([halo, own], 0),
            wq_t=np.ascontiguousarray(np.asarray(L0_attn_Wq, np.float32).T),
            wk_t=np.ascontiguousarray(np.asarray(L0_attn_Wk, np.float32).T),
            wv_t=np.ascontiguousarray(np.asarray(L0_attn_Wv, np.float32).T),
            wo_t=np.ascontiguousarray(np.asarray(L0_attn_Wo, np.float32).T),
            wg_in=np.ascontiguousarray(np.asarray(L0_ffn_w_gate, np.float32)),
            wu_in=np.ascontiguousarray(np.asarray(L0_ffn_w_up, np.float32)),
            wd_in=np.ascontiguousarray(np.asarray(L0_ffn_w_down, np.float32)),
            qg_in=np.asarray(L0_attn_q_gain, np.float32).reshape(1, H),
            pm_in=np.asarray(L0_attn_pair_mix, np.float32).reshape(1, 32),
            cos_in=np.ascontiguousarray(cos_f[pos]),
            sin_in=np.ascontiguousarray(sin_f[pos]),
            msk_in=_masks(q == 0),
            idf_in=np.eye(128, dtype=np.float32),
            idb_in=np.eye(128).astype(ml_dtypes.bfloat16),
        )
        in_maps.append(m)
    res = run_bass_kernel_spmd(nc, in_maps, core_ids=list(range(NCORE)))
    out = np.empty((Bb, S, D), np.float32)
    for c in range(NCORE):
        b, q = c // 4, c % 4
        out[b, q * TO:(q + 1) * TO] = res.results[c]["y_out"]
    return out
